# version 18
# baseline (speedup 1.0000x reference)
import math
from contextlib import ExitStack

import numpy as np
import ml_dtypes

import concourse.bass as bass
import concourse.mybir as mybir
from concourse.bass_core_rust import IndirectOffsetOnAxis
from concourse.bass_utils import run_bass_kernel_spmd

F32 = mybir.dt.float32
BF16 = mybir.dt.bfloat16
U32 = mybir.dt.uint32
AF = mybir.ActivationFunctionType
ALU = mybir.AluOpType
AX = mybir.AxisListType.X
BF = ml_dtypes.bfloat16

Bsz, S, D = 4, 2048, 4096
NCORE = 8
NTILE, TOK = 8, 128
NTOK = NTILE * TOK
L = D
NF = L // 2 + 1
FC = 17
NFP = FC * 128
DC = D // 128
HEADS, KVH, HD = 32, 8, 128
GRP = HEADS // KVH
QDIM, KVDIM = HEADS * HD, KVH * HD
QKVD = QDIM + 2 * KVDIM
ROPE_THETA = 10000.0
ATT_SCALE = 1.0 / math.sqrt(HD)
MAXNEG = -1e20

B2_COLS, B2_NBLK = 4608, 9
OFF_H0, OFF_GATE, OFF_THR = 4096, 4224, 4232
QKV_COLS, QKV_NBLK = 6656, 13
OFF_H1 = 6144

DBG = {}


class Buf:
    __slots__ = ("name", "writer", "readers", "excl")

    def __init__(self, name, excl=False):
        self.name = name
        self.writer = None
        self.readers = []
        self.excl = excl


class Em:
    QUEUES = ("pe", "dve", "act", "sp", "pool")

    def __init__(self, nc, es):
        self.nc = nc
        self.tape = []
        self.sem = {}
        self.cnt = {}
        self.waited = {}
        for q in self.QUEUES:
            self.sem[q] = es.enter_context(nc.semaphore("q_" + q))
            self.cnt[q] = 0
        for d in ("dsp0", "dsp1", "dsp2", "dsp3", "dw0", "dw1", "dw2", "dw3",
                  "dg0", "dg1", "dg2", "dg3", "dst0", "dst1", "dcc"):
            self.sem[d] = es.enter_context(nc.semaphore("s_" + d))
            self.cnt[d] = 0

    def _emit_wait(self, q, semname, val):
        if val is None or val <= 0:
            return
        key = (q, semname)
        if self.waited.get(key, 0) >= val:
            return
        self.waited[key] = val
        h = self.sem[semname]
        self.tape.append((q, lambda e, h=h, v=val: e.wait_ge(h, v)))

    def _dep_waits(self, q, reads, writes):
        ev = []
        for b in reads:
            if b.writer is not None:
                ev.append(b.writer)
            if b.excl:
                ev.extend(b.readers)
        for b in writes:
            if b.writer is not None:
                ev.append(b.writer)
            ev.extend(b.readers)
        need = {}
        for semname, val in ev:
            if semname == "pe" and q == "pe":
                continue
            if val is None:
                raise RuntimeError(f"dependence on unfinished group (queue {q})")
            need[semname] = max(need.get(semname, 0), val)
        for semname, val in sorted(need.items()):
            self._emit_wait(q, semname, val)

    def _post(self, ev, reads, writes):
        for b in writes:
            b.writer = ev
            b.readers = []
        for b in reads:
            b.readers.append(ev)

    def op(self, q, fn, reads=(), writes=()):
        assert q in ("pe", "dve", "act", "pool")
        self._dep_waits(q, reads, writes)
        self.cnt[q] += 1
        h = self.sem[q]

        def run(e, fn=fn, h=h):
            fn(e).then_inc(h, 1)
        self.tape.append((q, run))
        ev = (q, self.cnt[q])
        self._post(ev, reads, writes)
        return ev

    def dma(self, q, sem, fn, reads=(), writes=()):
        assert q in ("sp", "pool")
        self._dep_waits(q, reads, writes)
        self._emit_wait(q, sem, self.cnt[sem])
        self.cnt[sem] += 16
        h = self.sem[sem]

        def run(e, fn=fn, h=h):
            fn(e).then_inc(h, 16)
        self.tape.append((q, run))
        ev = (sem, self.cnt[sem])
        self._post(ev, reads, writes)
        return ev

    def mm_group(self, mms, reads=(), writes=()):
        self._dep_waits("pe", reads, writes)
        self.cnt["pe"] += 1
        h = self.sem["pe"]
        n = len(mms)
        for i, fn in enumerate(mms):
            if i < n - 1:
                self.tape.append(("pe", lambda e, fn=fn: fn(e)))
            else:
                def run(e, fn=fn, h=h):
                    fn(e).then_inc(h, 1)
                self.tape.append(("pe", run))
        ev = ("pe", self.cnt["pe"])
        self._post(ev, reads, writes)
        return ev

    def barrier(self, qs=None):
        qs = qs or self.QUEUES
        snap = dict(self.cnt)
        for q in qs:
            for semname, val in snap.items():
                if val > 0 and semname != q:
                    self._emit_wait(q, semname, val)

    def replay(self, block):
        per = {q: [it for (qq, it) in self.tape if qq == q] for q in self.QUEUES}

        def section(items):
            def f(eng):
                for it in items:
                    it(eng)
            return f
        block.tensor(section(per["pe"]))
        block.vector(section(per["dve"]))
        block.scalar(section(per["act"]))
        block.sync(section(per["sp"]))
        block.gpsimd(section(per["pool"]))


_DTSIZE = {F32: 4, BF16: 2, U32: 4}


class Arena:

    BASE = 16384

    def __init__(self, nc, limit=207 * 1024 + 512):
        self.nc = nc
        self.limit = self.BASE + limit
        self.free_list = [(self.BASE, limit)]
        self.n = 0
        self.sizes = {}

    def alloc(self, shape, dtype, name=None):
        per = int(np.prod(shape[1:])) * _DTSIZE[dtype]
        per = (per + 31) & ~31
        name = name or f"sb{self.n}"
        self.n += 1
        for i, (off, sz) in enumerate(self.free_list):
            if sz >= per:
                h = self.nc.alloc_sbuf_tensor_at(name, list(shape), dtype, offset=off)
                if sz == per:
                    self.free_list.pop(i)
                else:
                    self.free_list[i] = (off + per, sz - per)
                self.sizes[id(h)] = (off, per)
                return h
        raise MemoryError(f"SBUF arena overflow: need {per} for {name}; free={self.free_list}")

    def free(self, *handles):
        for h in handles:
            off, per = self.sizes.pop(id(h))
            self.free_list.append((off, per))
        self.free_list.sort()
        merged = []
        for off, sz in self.free_list:
            if merged and merged[-1][0] + merged[-1][1] == off:
                merged[-1] = (merged[-1][0], merged[-1][1] + sz)
            else:
                merged.append((off, sz))
        self.free_list = [tuple(x) for x in merged]


_TABLES = None


def host_tables():
    global _TABLES
    if _TABLES is not None:
        return _TABLES
    i = np.arange(L, dtype=np.float64)
    f = np.arange(NFP, dtype=np.float64)
    ang = np.outer(i, f) * (2.0 * np.pi / L)
    cosA = np.cos(ang)
    sinA = np.sin(ang)
    cosA[:, NF:] = 0.0
    sinA[:, NF:] = 0.0
    wf = np.zeros(NFP)
    wf[0] = 1.0 / L
    wf[1:NF - 1] = 2.0 / L
    wf[NF - 1] = 1.0 / L
    angB = np.outer(f, i) * (2.0 * np.pi / L)
    cosB = np.cos(angB) * wf[:, None]
    sinB = np.sin(angB) * wf[:, None]
    cosB[NF:, :] = 0.0
    sinB[NF:, :] = 0.0
    inv = 1.0 / ROPE_THETA ** (np.arange(0, HD, 2, dtype=np.float64) / HD)
    a = np.outer(np.arange(S, dtype=np.float64), inv)
    rc = np.tile(np.cos(a), (1, HEADS)).astype(BF)
    rs = np.tile(np.sin(a), (1, HEADS)).astype(BF)
    q = np.arange(128)[:, None]
    k = np.arange(512)[None, :]
    masks = np.stack([np.where(k <= q + 128 * m, 0.0, MAXNEG) for m in range(4)], 1)
    ident = np.eye(128, dtype=np.float32).astype(BF)
    def tile_A(tab):
        return np.ascontiguousarray(
            tab.reshape(DC, 128, FC, 128).transpose(2, 1, 0, 3).reshape(FC, 128, DC * 128))

    def tile_B(tab):
        return np.ascontiguousarray(
            tab.reshape(FC, 128, 8, 512).transpose(2, 1, 0, 3).reshape(8, 128, FC * 512))

    cosA16, sinA16 = cosA.astype(BF), sinA.astype(BF)
    cosB16, sinB16 = cosB.astype(BF), sinB.astype(BF)
    _TABLES = dict(
        cosA=np.ascontiguousarray(cosA16), sinA=np.ascontiguousarray(sinA16),
        cosA_t=tile_A(cosA16), sinA_t=tile_A(sinA16),
        cosB_t=tile_B(cosB16), sinB_t=tile_B(sinB16),
        rope_cos=np.ascontiguousarray(rc), rope_sin=np.ascontiguousarray(rs),
        masks=np.ascontiguousarray(masks.reshape(128, 4 * 512).astype(np.float32)),
        ident=ident,
    )
    return _TABLES


def host_weights(inp):
    w = {}
    cat = np.concatenate([
        np.asarray(inp["L0_ffn_W_base"]).T,
        np.asarray(inp["L0_ffn_A_w"]).reshape(128, D).T,
        np.asarray(inp["L0_ffn_W_gate"]).T,
        np.asarray(inp["L0_ffn_W_thr"]).reshape(D, 1),
        np.zeros((D, B2_COLS - 4233), np.float32),
    ], axis=1)
    w["w_b2"] = np.ascontiguousarray(
        cat.astype(np.float32).reshape(DC, 128, B2_NBLK, 512).transpose(2, 1, 0, 3)
        .reshape(B2_NBLK, 128, DC * 512))
    b2 = np.zeros((1, B2_COLS), np.float32)
    b2[0, :4096] = np.asarray(inp["L0_ffn_b_base"])
    b2[0, OFF_THR] = np.asarray(inp["L0_ffn_b_thr"])[0]
    w["b_b2"] = b2
    w["w_B0"] = np.ascontiguousarray(np.asarray(inp["L0_ffn_B_w"]).transpose(0, 2, 1).reshape(128, D), np.float32)
    qcat = np.concatenate([
        np.asarray(inp["L1_attn_w_qkv"]).T,
        np.asarray(inp["L1_ffn_lora_A"]).reshape(128, D).T,
        np.zeros((D, QKV_COLS - 6272), np.float32),
    ], axis=1)
    w["w_qkv"] = np.ascontiguousarray(
        qcat.astype(np.float32).reshape(DC, 128, QKV_NBLK, 512).transpose(2, 1, 0, 3)
        .reshape(QKV_NBLK, 128, DC * 512))
    qb = np.zeros((1, QKV_COLS), np.float32)
    qb[0, :QKVD] = np.asarray(inp["L1_attn_b_qkv"])
    w["b_qkv"] = qb
    w["w_wo"] = np.ascontiguousarray(np.asarray(inp["L1_attn_w_o"]).T.astype(np.float32).reshape(DC, 128, 8, 512).transpose(2, 1, 0, 3).reshape(8, 128, DC * 512))
    w["b_wo"] = np.ascontiguousarray(np.asarray(inp["L1_attn_b_o"]).reshape(1, D), np.float32)
    w["w_w1"] = np.ascontiguousarray(np.asarray(inp["L1_ffn_W"]).T.astype(np.float32).reshape(DC, 128, 8, 512).transpose(2, 1, 0, 3).reshape(8, 128, DC * 512))
    w["b_w1"] = np.ascontiguousarray(np.asarray(inp["L1_ffn_bias"]).reshape(1, D), np.float32)
    w["w_B1"] = np.ascontiguousarray(np.asarray(inp["L1_ffn_lora_B"]).transpose(0, 2, 1).reshape(128, D), np.float32)
    w["lora_w"] = np.ascontiguousarray(np.asarray(inp["L1_ffn_lora_weights"]).reshape(1, 8), np.float32)
    return w


def shard_rows(core):
    return core // 2, [core % 2 + 2 * t for t in range(NTILE)]


def shard_x(x, core):
    s, tiles = shard_rows(core)
    xr = np.asarray(x[s]).reshape(16, 128, D)
    return np.ascontiguousarray(xr[tiles].reshape(NTOK, D))


def unshard(outs):
    full = np.zeros((Bsz, S, D), np.float32)
    for c in range(NCORE):
        s, tiles = shard_rows(c)
        o = np.asarray(outs[c]).reshape(NTILE, 128, D)
        fr = full[s].reshape(16, 128, D)
        for i, t in enumerate(tiles):
            fr[t] = o[i]
    return full


class Prog:
    def __init__(self, nc, es, core):
        self.nc, self.es, self.core = nc, es, core
        self.em = Em(nc, es)
        self.ar = Arena(nc)
        dr = nc.dram_tensor
        self.d_ident = dr("ident", [128, 128], BF16, kind="ExternalInput").ap()
        self.d_identf = dr("identf", [128, 128], F32, kind="ExternalInput").ap()
        self.ident = self.ar.alloc([128, 128], BF16, "ident")
        self.b_ident = Buf("ident")
        self.identf = self.ar.alloc([128, 128], F32, "identf")
        self.b_identf = Buf("identf")
        self.ones1 = self.ar.alloc([1, 128], BF16, "ones1")
        self.b_ones = Buf("ones1")
        self.zero1 = self.ar.alloc([128, 1], F32, "zero1")
        self.b_zero1 = Buf("zero1")
        self.eps1 = self.ar.alloc([128, 1], F32, "eps1")
        self.b_eps1 = Buf("eps1")
        self.ps_acc = nc.alloc_psum_tensor("ps_acc", [128, 2048], F32)
        self.ps_b4 = nc.alloc_psum_tensor("ps_b4", [128, 512], F32)
        self.ps_b5 = nc.alloc_psum_tensor("ps_b5", [128, 512], F32)
        self.ps_t0 = nc.alloc_psum_tensor("ps_t0", [128, 1024], BF16)
        self.ps_t1 = nc.alloc_psum_tensor("ps_t1", [128, 1024], BF16)
        self.b_acc = [Buf(f"acc{i}", excl=True) for i in range(4)]
        self.b_b4, self.b_b5 = Buf("b4", excl=True), Buf("b5", excl=True)
        self.b_t = [Buf("t0", excl=True), Buf("t1", excl=True)]
        self.dbg = {}

    def prelude(self):
        em = self.em
        em.dma("sp", "dsp0", lambda e: e.dma_start(out=self.ident[:], in_=self.d_ident),
               writes=[self.b_ident])
        em.dma("sp", "dsp0", lambda e: e.dma_start(out=self.identf[:], in_=self.d_identf),
               writes=[self.b_identf])
        em.op("dve", lambda e: e.memset(self.ones1[:], 1.0), writes=[self.b_ones])
        em.op("dve", lambda e: e.memset(self.zero1[:], 0.0), writes=[self.b_zero1])
        em.op("dve", lambda e: e.memset(self.eps1[:], 1e-5), writes=[self.b_eps1])

    def tap(self, name, ap, shape, dtype, reads, force=False):
        if not (force or DBG.get(name)):
            return
        key = "dbg_" + name
        d = self.nc.dram_tensor(key, list(shape), dtype, kind="ExternalOutput").ap()
        b = Buf(key)
        self.dbg[key] = b
        self.em.dma("sp", "dst0", lambda e: e.dma_start(out=d, in_=ap), reads=list(reads), writes=[b])

    def trans128(self, src_ap, dst_ap, b_src, b_dst, idx, f32=False, copy_eng="dve"):
        em = self.em
        if f32:
            ps, tb = (self.ps_b4, self.b_b4) if idx % 2 == 0 else (self.ps_b5, self.b_b5)
            psl = ps[:, 0:128]
            idap, bid = self.identf[:], self.b_identf
        else:
            ps, tb = (self.ps_t0, self.b_t[0]) if idx % 2 == 0 else (self.ps_t1, self.b_t[1])
            psl = ps[:, 0:128]
            idap, bid = self.ident[:], self.b_ident
        em.op("pe", lambda e: e.transpose(psl, src_ap, idap),
              reads=[b_src, bid], writes=[tb])
        em.op(copy_eng, lambda e: (e.tensor_copy(out=dst_ap, in_=psl) if copy_eng != "act"
                                   else e.activation(out=dst_ap, in_=psl, func=AF.Identity, bias=self.zero1[:])),
              reads=[tb] + ([self.b_zero1] if copy_eng == "act" else []), writes=[b_dst])


def layernorm_tile(p, src_ap, b_src, out_ap, b_out, scratch):
    em = p.em
    stats, b_stats, mv, b_mv, rstd, b_rstd, nbias, b_nbias = scratch
    for c8 in range(8):
        em.op("dve", lambda e, c8=c8: e.bn_stats(
            out=stats[:, 6 * c8:6 * c8 + 6], in_=src_ap[:, 512 * c8:512 * (c8 + 1)]),
            reads=[b_src], writes=[b_stats])
    em.op("dve", lambda e: e.bn_aggr(out=mv[:], in_=stats[:]), reads=[b_stats], writes=[b_mv])
    em.op("act", lambda e: e.activation(out=rstd[:], in_=mv[:, 1:2], func=AF.Ln, bias=p.eps1[:]),
          reads=[b_mv, p.b_eps1], writes=[b_rstd])
    em.op("act", lambda e: e.activation(out=rstd[:], in_=rstd[:], func=AF.Exp, scale=-0.5, bias=p.zero1[:]),
          reads=[b_rstd, p.b_zero1], writes=[b_rstd])
    em.op("dve", lambda e: e.scalar_tensor_tensor(
        out=nbias[:], in0=mv[:, 0:1], scalar=-1.0, in1=rstd[:], op0=ALU.mult, op1=ALU.mult),
        reads=[b_mv, b_rstd], writes=[b_nbias])
    em.op("act", lambda e: e.activation(
        out=out_ap, in_=src_ap, func=AF.Identity, bias=nbias[:], scale=rstd[:]),
        reads=[b_src, b_rstd, b_nbias], writes=[b_out])


def build_l0(p):
    nc, em, ar = p.nc, p.em, p.ar

    dr = nc.dram_tensor
    p.d_x = dr("x_in", [NTOK, D], F32, kind="ExternalInput").ap()
    p.d_cosA = dr("cosA", [L, NFP], BF16, kind="ExternalInput").ap()
    p.d_sinA = dr("sinA", [L, NFP], BF16, kind="ExternalInput").ap()
    p.d_cosA_t = dr("cosA_t", [FC, 128, DC * 128], BF16, kind="ExternalInput").ap()
    p.d_sinA_t = dr("sinA_t", [FC, 128, DC * 128], BF16, kind="ExternalInput").ap()
    p.d_cosB_t = dr("cosB_t", [8, 128, FC * 512], BF16, kind="ExternalInput").ap()
    p.d_sinB_t = dr("sinB_t", [8, 128, FC * 512], BF16, kind="ExternalInput").ap()
    ext = "ExternalOutput" if DBG.get("l0_out") else "Internal"
    p.d_x1 = nc.dram_tensor("x1_d", [NTOK, D], F32, kind=ext).ap()
    p.b_x1d = Buf("x1_d")
    p.d_base0 = nc.dram_tensor("base0_d", [NTOK, D], BF16, kind="Internal").ap()
    p.b_base0d = Buf("base0_d")
    p.d_wb2 = nc.dram_tensor("w_b2", [B2_NBLK, 128, DC * 512], F32, kind="ExternalInput").ap()
    p.d_bb2 = nc.dram_tensor("b_b2", [1, B2_COLS], F32, kind="ExternalInput").ap()
    p.d_B0 = nc.dram_tensor("w_B0", [128, D], F32, kind="ExternalInput").ap()

    h0 = ar.alloc([128, NTILE, 128], F32, "h0")
    b_h0 = Buf("h0")
    gate = ar.alloc([128, NTILE, 8], F32, "gate")
    b_gate = Buf("gate")
    thr = ar.alloc([128, NTILE, 1], F32, "thr")
    b_thr = Buf("thr")
    m8 = ar.alloc([128, NTILE, 8], F32, "m8")
    b_m8 = Buf("m8")
    i8 = ar.alloc([128, NTILE, 8], U32, "i8")
    b_i8 = Buf("i8")
    wn = ar.alloc([128, NTILE, 8], F32, "wn")
    b_wn = Buf("wn")
    lnscratch = (ar.alloc([128, 48], F32, "stats"), Buf("stats"),
                 ar.alloc([128, 2], F32, "mv"), Buf("mv"),
                 ar.alloc([128, 1], F32, "rstd"), Buf("rstd"),
                 ar.alloc([128, 1], F32, "nbias"), Buf("nbias"))
    scr = (lnscratch[0], lnscratch[1], lnscratch[2], lnscratch[3],
           lnscratch[4], lnscratch[5], lnscratch[6], lnscratch[7])

    xnT = ar.alloc([128, DC, NTOK], BF16, "xnT")
    b_xnT = Buf("xnT")
    xa = ar.alloc([128, 2, D], F32, "xtile")
    b_xa = [Buf("xa0"), Buf("xa1")]
    xn = ar.alloc([128, D], BF16, "xn")
    b_xn = Buf("xn")
    for ti in range(NTILE):
        xab, b_x = xa[:, ti % 2, :], b_xa[ti % 2]
        em.dma("sp", "dsp" + str(ti % 2), lambda e, ti=ti, xab=xab: e.dma_start(
            out=xab, in_=p.d_x[ti * 128:(ti + 1) * 128, :]), writes=[b_x])
        layernorm_tile(p, xab, b_x, xn[:], b_xn, scr)
        for dc in range(DC):
            p.trans128(xn[:, dc * 128:(dc + 1) * 128],
                       xnT[:, dc, ti * 128:(ti + 1) * 128], b_xn, b_xnT, dc)
    ar.free(xa, xn)
    em.barrier()
    if DBG.get("stop_after_A"):
        p.tap("xnT_A", xnT[:], [128, DC * NTOK], BF16, [b_xnT], force=True)
        em.barrier()
        return
    if DBG.get("xnT"):
        p.tap("xnT", xnT[:], [128, DC * NTOK], BF16, [b_xnT])

    wblk = ar.alloc([128, 2, DC, 512], BF16, "wblk")
    b_w = [Buf("w0"), Buf("w1")]
    brow = ar.alloc([1, B2_COLS], BF16, "brow")
    b_brow = Buf("brow")
    evac = ar.alloc([128, 2, 512], BF16, "evac")
    b_evac = [Buf("ev0"), Buf("ev1")]
    em.dma("pool", "dw2", lambda e: e.dma_start(out=brow[:], in_=p.d_bb2), writes=[b_brow])
    for blk in range(DBG.get("b2_blocks", B2_NBLK)):
        wb, bw = wblk[:, blk % 2], b_w[blk % 2]
        em.dma("pool", "dw" + str(blk % 2), lambda e, blk=blk, wb=wb: e.dma_start(
            out=wb, in_=p.d_wb2[blk].rearrange("p (c n) -> p c n", c=DC)),
            writes=[bw])
        for ti in range(NTILE):
            rb4 = (blk * NTILE + ti) % 4
            bank = p.b_acc[rb4]
            psl = p.ps_acc[:, rb4 * 512:(rb4 + 1) * 512]
            mms = []
            for kc in range(DC):
                mms.append(lambda e, kc=kc, ti=ti, wb=wb, psl=psl: e.matmul(
                    psl, xnT[:, kc, ti * 128:(ti + 1) * 128], wb[:, kc, :],
                    start=(kc == 0), stop=False))
            if DBG.get("no_bias"):
                mms[-1] = (lambda f: (lambda e: f(e, True)))(
                    (lambda e, kc=DC - 1, ti=ti, wb=wb, psl=psl: None)) if False else mms[-1]
                kc = DC - 1
                mms[-1] = lambda e, kc=kc, ti=ti, wb=wb, psl=psl: e.matmul(
                    psl, xnT[:, kc, ti * 128:(ti + 1) * 128], wb[:, kc, :],
                    start=False, stop=True)
            else:
                mms.append(lambda e, blk=blk, psl=psl: e.matmul(
                    psl, p.ones1[:], brow[:, blk * 512:(blk + 1) * 512], start=False, stop=True))
            em.mm_group(mms, reads=[b_xnT, bw, p.b_ones, b_brow], writes=[bank])
            if blk < 8:
                ev, bev = evac[:, ti % 2, :], b_evac[ti % 2]
                em.op("act", lambda e, psl=psl, ev=ev: e.activation(out=ev, in_=psl, func=AF.Identity, bias=p.zero1[:]),
                      reads=[bank], writes=[bev])
                em.dma("sp", "dst" + str(ti % 2), lambda e, blk=blk, ti=ti, ev=ev: e.dma_start(
                    out=p.d_base0[ti * 128:(ti + 1) * 128, blk * 512:(blk + 1) * 512], in_=ev),
                    reads=[bev], writes=[p.b_base0d])
            else:
                if not DBG.get("skip_h0"):
                    em.op("act", lambda e, ti=ti, psl=psl: e.activation(
                        out=h0[:, ti, :], in_=psl[:, 0:128], func=AF.Identity, bias=p.zero1[:]),
                        reads=[bank], writes=[b_h0])
                if not DBG.get("skip_gate"):
                    em.op("dve", lambda e, ti=ti, psl=psl: e.tensor_copy(
                        out=gate[:, ti, :], in_=psl[:, 128:136]), reads=[bank], writes=[b_gate])
                if not DBG.get("skip_thr"):
                    em.op("dve", lambda e, ti=ti, psl=psl: e.tensor_copy(
                        out=thr[:, ti, :], in_=psl[:, 136:137]), reads=[bank], writes=[b_thr])
    ar.free(wblk, brow, evac)
    em.barrier()
    if DBG.get("stop_after_B2"):
        p.tap("h0_B2", h0[:], [128, NTILE * 128], F32, [b_h0], force=True)
        em.barrier()
        return

    RI = ar.alloc([128, 2, FC, NTOK], BF16, "RI")
    b_RI = Buf("RI")
    tabA = ar.alloc([128, 2, 2, DC, 128], BF16, "tabA")
    b_tA = [Buf("tA0"), Buf("tA1")]
    for fb in range(FC):
        tb, btb = tabA[:, fb % 2], b_tA[fb % 2]
        em.dma("pool", "dw" + str(2 * (fb % 2)), lambda e, fb=fb, tb=tb: e.dma_start(
            out=tb[:, 0], in_=p.d_cosA_t[fb].rearrange("p (c n) -> p c n", c=DC)),
            writes=[btb])
        em.dma("pool", "dw" + str(2 * (fb % 2) + 1), lambda e, fb=fb, tb=tb: e.dma_start(
            out=tb[:, 1], in_=p.d_sinA_t[fb].rearrange("p (c n) -> p c n", c=DC)),
            writes=[btb])
        for cs in range(2):
            for th in range(2):
                rb4 = (fb * 4 + cs * 2 + th) % 4
                bank = p.b_acc[rb4]
                psl = p.ps_acc[:, rb4 * 512:(rb4 + 1) * 512]
                mms = []
                for kc in range(DC):
                    mms.append(lambda e, kc=kc, cs=cs, th=th, tb=tb, psl=psl: e.matmul(
                        psl, tb[:, cs, kc, :], xnT[:, kc, th * 512:(th + 1) * 512],
                        start=(kc == 0), stop=(kc == DC - 1)))
                em.mm_group(mms, reads=[b_xnT, btb], writes=[bank])
                em.op("act", lambda e, fb=fb, cs=cs, th=th, psl=psl: e.activation(
                    out=RI[:, cs, fb, th * 512:(th + 1) * 512], in_=psl, func=AF.Identity, bias=p.zero1[:]),
                    reads=[bank], writes=[b_RI])
    if DBG.get("RI"):
        p.tap("RI", RI[:], [128, 2 * FC * NTOK], BF16, [b_RI])
    ar.free(tabA, xnT)
    em.barrier()
    if DBG.get("stop_after_B1"):
        ar.free(RI)
        em.barrier()
        return

    P = ar.alloc([128, FC, NTOK], BF16, "P")
    b_P = Buf("P")
    ptmp = ar.alloc([128, NTOK], F32, "ptmp")
    b_pt = Buf("ptmp")
    for fcc in range(FC):
        em.op("dve", lambda e, fcc=fcc: e.tensor_tensor(
            out=ptmp[:], in0=RI[:, 0, fcc, :], in1=RI[:, 0, fcc, :], op=ALU.mult),
            reads=[b_RI], writes=[b_pt])
        em.op("dve", lambda e, fcc=fcc: e.scalar_tensor_tensor(
            out=P[:, fcc, :], in0=RI[:, 1, fcc, :], scalar=1.0, in1=RI[:, 1, fcc, :],
            op0=ALU.mult, op1=ALU.mult),
            reads=[b_RI], writes=[b_P])
        em.op("dve", lambda e, fcc=fcc: e.tensor_tensor(
            out=P[:, fcc, :], in0=P[:, fcc, :], in1=ptmp[:], op=ALU.add),
            reads=[b_pt, b_P], writes=[b_P])
    ar.free(ptmp)
    em.barrier()

    corr = ar.alloc([128, NTILE, L], BF16, "corr")
    b_corr = Buf("corr")
    tabB = ar.alloc([128, FC, 512], BF16, "tabB")
    b_tB = Buf("tB")
    for db in range(8):
        tb, btb = tabB, b_tB
        em.dma("pool", "dw0", lambda e, db=db, tb=tb: e.dma_start(
            out=tb[:], in_=p.d_cosB_t[db].rearrange("p (c n) -> p c n", c=FC)),
            writes=[btb])
        for ti in range(NTILE):
            rb4 = (db * NTILE + ti) % 4
            bank = p.b_acc[rb4]
            psl = p.ps_acc[:, rb4 * 512:(rb4 + 1) * 512]
            mms = []
            for fcc in range(FC):
                mms.append(lambda e, fcc=fcc, ti=ti, tb=tb, psl=psl: e.matmul(
                    psl, P[:, fcc, ti * 128:(ti + 1) * 128], tb[:, fcc, :],
                    start=(fcc == 0), stop=(fcc == FC - 1)))
            em.mm_group(mms, reads=[b_P, btb], writes=[bank])
            em.op("act", lambda e, db=db, ti=ti, psl=psl: e.activation(
                out=corr[:, ti, db * 512:(db + 1) * 512], in_=psl, func=AF.Identity, bias=p.zero1[:]),
                reads=[bank], writes=[b_corr])
    ar.free(P, tabB)
    em.barrier()
    if DBG.get("corr"):
        p.tap("corr", corr[:], [128, NTILE * L], BF16, [b_corr])

    sm = ar.alloc([128, 8], F32, "smtmp")
    b_sm = Buf("smtmp")
    nv0 = ar.alloc([128, 1], F32, "nv0")
    b_nv0 = Buf("nv0")
    wsum = ar.alloc([128, 1], F32, "wsum")
    b_ws = Buf("wsum")
    winv = ar.alloc([128, 1], F32, "winv")
    b_wi = Buf("winv")
    def topk_tile(ti):
        em.op("dve", lambda e, ti=ti: e.max(out=m8[:, ti, :], in_=corr[:, ti, :]),
              reads=[b_corr], writes=[b_m8])
        yield
        em.op("dve", lambda e, ti=ti: e.max_index(out=i8[:, ti, :], in_max=m8[:, ti, :], in_values=corr[:, ti, :]),
              reads=[b_corr, b_m8], writes=[b_i8])
        yield
        em.op("dve", lambda e, ti=ti: e.tensor_scalar(
            out=nv0[:], in0=m8[:, ti, 0:1], scalar1=-1.0, scalar2=None, op0=ALU.mult),
            reads=[b_m8], writes=[b_nv0])
        em.op("act", lambda e, ti=ti: e.activation(
            out=sm[:], in_=m8[:, ti, :], func=AF.Exp, bias=nv0[:], accum_out=wsum[:]),
            reads=[b_m8, b_nv0], writes=[b_sm, b_ws])
        em.op("dve", lambda e: e.reciprocal(out=winv[:], in_=wsum[:]), reads=[b_ws], writes=[b_wi])
        em.op("dve", lambda e, ti=ti: e.tensor_scalar(
            out=wn[:, ti, :], in0=sm[:], scalar1=winv[:], scalar2=None, op0=ALU.mult),
            reads=[b_sm, b_wi], writes=[b_wn])
        yield
    tk_live = [topk_tile(0), topk_tile(1)]
    tk_next = 2
    while tk_live:
        keep = []
        for g in tk_live:
            try:
                next(g)
                keep.append(g)
            except StopIteration:
                pass
        tk_live = keep
        if len(tk_live) < 2 and tk_next < NTILE:
            tk_live.append(topk_tile(tk_next))
            tk_next += 1
    ar.free(sm, nv0, wsum, winv, corr)
    em.barrier()
    if DBG.get("topk"):
        p.tap("topk_m8", m8[:], [128, NTILE * 8], F32, [b_m8], force=True)
        p.tap("topk_i8", i8[:], [128, NTILE * 8], U32, [b_i8], force=True)
        p.tap("topk_wn", wn[:], [128, NTILE * 8], F32, [b_wn], force=True)

    Y = ar.alloc([128, 2, FC, NTOK], BF16, "Y")
    b_Y = Buf("Y")
    Gc = ar.alloc([128, 2, 2, NFP], BF16, "Gc")
    Gs = ar.alloc([128, 2, 2, NFP], BF16, "Gs")
    b_G = [[Buf(f"g{t}{i}") for i in range(2)] for t in range(2)]
    Kc = ar.alloc([128, 2, NFP], BF16, "Kc")
    Ks = ar.alloc([128, 2, NFP], BF16, "Ks")
    b_Kc = [Buf("Kc0"), Buf("Kc1")]
    b_Ks = [Buf("Ks0"), Buf("Ks1")]
    KT = ar.alloc([128, 1, 2, FC, 128], BF16, "KT")
    b_KT = [Buf("KT0"), Buf("KT0b")]
    ytmp = ar.alloc([128, 2, 2, 128], F32, "ytmp")
    b_yt = [[Buf(f"yt{t}{i}") for i in range(2)] for t in range(2)]

    def mixer_tile(ti):
        tp = ti % 2
        bKc, bKs, bKT = b_Kc[tp], b_Ks[tp], b_KT[0]
        kc_t, ks_t = Kc[:, tp, :], Ks[:, tp, :]
        ktt = KT[:, 0]
        for j in range(8):
            gp = j % 2
            gc, gs, bg = Gc[:, tp, gp, :], Gs[:, tp, gp, :], b_G[tp][gp]
            em.dma("pool", "dg" + str(2 * tp), lambda e, ti=ti, j=j, gc=gc: e.indirect_dma_start(
                out=gc, out_offset=None, in_=p.d_cosA,
                in_offset=IndirectOffsetOnAxis(ap=i8[:, ti, j:j + 1], axis=0)),
                reads=[b_i8], writes=[bg])
            em.dma("pool", "dg" + str(2 * tp + 1), lambda e, ti=ti, j=j, gs=gs: e.indirect_dma_start(
                out=gs, out_offset=None, in_=p.d_sinA,
                in_offset=IndirectOffsetOnAxis(ap=i8[:, ti, j:j + 1], axis=0)),
                reads=[b_i8], writes=[bg])
            em.op("act", lambda e, ti=ti, j=j, gc=gc: e.activation(
                out=gc, in_=gc, func=AF.Identity, scale=wn[:, ti, j:j + 1], bias=p.zero1[:]),
                reads=[b_wn, p.b_zero1, bg], writes=[bg])
            em.op("act", lambda e, ti=ti, j=j, gs=gs: e.activation(
                out=gs, in_=gs, func=AF.Identity, scale=wn[:, ti, j:j + 1], bias=p.zero1[:]),
                reads=[b_wn, p.b_zero1, bg], writes=[bg])
            if j == 0:
                em.op("dve", lambda e, gc=gc, kc_t=kc_t: e.tensor_copy(out=kc_t, in_=gc),
                      reads=[bg], writes=[bKc])
                em.op("dve", lambda e, gs=gs, ks_t=ks_t: e.tensor_copy(out=ks_t, in_=gs),
                      reads=[bg], writes=[bKs])
            else:
                em.op("dve", lambda e, gc=gc, kc_t=kc_t: e.tensor_tensor(
                    out=kc_t, in0=kc_t, in1=gc, op=ALU.add), reads=[bg, bKc], writes=[bKc])
                em.op("dve", lambda e, gs=gs, ks_t=ks_t: e.tensor_tensor(
                    out=ks_t, in0=ks_t, in1=gs, op=ALU.add), reads=[bg, bKs], writes=[bKs])
            if j % 2 == 1:
                yield
        for fcc in range(FC):
            p.trans128(kc_t[:, fcc * 128:(fcc + 1) * 128], ktt[:, 0, fcc, :], bKc, bKT, 2 * fcc + tp)
            p.trans128(ks_t[:, fcc * 128:(fcc + 1) * 128], ktt[:, 1, fcc, :], bKs, bKT, 2 * fcc + 1 + tp)
            if fcc % 4 == 3:
                yield
        for fcc in range(FC):
            yt0, byt0 = ytmp[:, tp, 0, :], b_yt[tp][0]
            yt1, byt1 = ytmp[:, tp, 1, :], b_yt[tp][1]
            em.op("dve", lambda e, fcc=fcc, ti=ti, ktt=ktt: e.tensor_tensor(
                out=Y[:, 0, fcc, ti * 128:(ti + 1) * 128],
                in0=RI[:, 0, fcc, ti * 128:(ti + 1) * 128], in1=ktt[:, 0, fcc, :], op=ALU.mult),
                reads=[b_RI, bKT], writes=[b_Y])
            em.op("dve", lambda e, fcc=fcc, ti=ti, ktt=ktt, yt0=yt0: e.tensor_tensor(
                out=yt0, in0=RI[:, 1, fcc, ti * 128:(ti + 1) * 128], in1=ktt[:, 1, fcc, :], op=ALU.mult),
                reads=[b_RI, bKT], writes=[byt0])
            em.op("dve", lambda e, fcc=fcc, ti=ti, yt0=yt0: e.tensor_tensor(
                out=Y[:, 0, fcc, ti * 128:(ti + 1) * 128],
                in0=Y[:, 0, fcc, ti * 128:(ti + 1) * 128], in1=yt0, op=ALU.add),
                reads=[byt0, b_Y], writes=[b_Y])
            em.op("dve", lambda e, fcc=fcc, ti=ti, ktt=ktt: e.tensor_tensor(
                out=Y[:, 1, fcc, ti * 128:(ti + 1) * 128],
                in0=RI[:, 1, fcc, ti * 128:(ti + 1) * 128], in1=ktt[:, 0, fcc, :], op=ALU.mult),
                reads=[b_RI, bKT], writes=[b_Y])
            em.op("dve", lambda e, fcc=fcc, ti=ti, ktt=ktt, yt1=yt1: e.tensor_tensor(
                out=yt1, in0=RI[:, 0, fcc, ti * 128:(ti + 1) * 128], in1=ktt[:, 1, fcc, :], op=ALU.mult),
                reads=[b_RI, bKT], writes=[byt1])
            em.op("dve", lambda e, fcc=fcc, ti=ti, yt1=yt1: e.tensor_tensor(
                out=Y[:, 1, fcc, ti * 128:(ti + 1) * 128],
                in0=Y[:, 1, fcc, ti * 128:(ti + 1) * 128], in1=yt1, op=ALU.subtract),
                reads=[byt1, b_Y], writes=[b_Y])
            if fcc % 4 == 3:
                yield

    t_live = [mixer_tile(0)]
    nxt_tile = 1
    while t_live:
        keep = []
        for g in t_live:
            try:
                next(g)
                keep.append(g)
            except StopIteration:
                pass
        t_live = keep
        if len(t_live) < 2 and nxt_tile < NTILE:
            t_live.append(mixer_tile(nxt_tile))
            nxt_tile += 1
    ar.free(Gc, Gs, Kc, Ks, KT, ytmp, RI)
    em.barrier()
    if DBG.get("Y"):
        p.tap("Y", Y[:], [128, 2 * FC * NTOK], BF16, [b_Y])

    B0 = ar.alloc([128, D], BF16, "B0")
    b_B0 = Buf("B0")
    em.dma("pool", "dw3", lambda e: e.dma_start(out=B0[:], in_=p.d_B0), writes=[b_B0])
    h0T = ar.alloc([128, NTILE, 128], BF16, "h0T")
    b_h0T = Buf("h0T")
    gm = ar.alloc([128, 1], F32, "gm")
    b_gm = Buf("gm")
    ge = ar.alloc([128, 8], F32, "ge")
    b_ge = Buf("ge")
    gs_ = ar.alloc([128, 1], F32, "gs")
    b_gs = Buf("gs")
    gi = ar.alloc([128, 1], F32, "gi")
    b_gi = Buf("gi")
    thrs = ar.alloc([128, 1], F32, "thrs")
    b_thrs = Buf("thrs")
    wm = ar.alloc([128, 8], F32, "wm")
    b_wm = Buf("wm")
    h0s = ar.alloc([128, 128], BF16, "h0s")
    b_h0s = Buf("h0s")
    for ti in range(NTILE):
        em.op("dve", lambda e, ti=ti: e.reduce_max(out=gm[:], in_=gate[:, ti, :], axis=AX, negate=True),
              reads=[b_gate], writes=[b_gm])
        em.op("act", lambda e, ti=ti: e.activation(
            out=ge[:], in_=gate[:, ti, :], func=AF.Exp, bias=gm[:], accum_out=gs_[:]),
            reads=[b_gate, b_gm], writes=[b_ge, b_gs])
        em.op("dve", lambda e: e.reciprocal(out=gi[:], in_=gs_[:]), reads=[b_gs], writes=[b_gi])
        em.op("act", lambda e, ti=ti: e.activation(out=thrs[:], in_=thr[:, ti, :], func=AF.Sigmoid, bias=p.zero1[:]),
              reads=[b_thr], writes=[b_thrs])
        em.op("dve", lambda e: e.tensor_scalar(
            out=thrs[:], in0=thrs[:], scalar1=-0.125, scalar2=None, op0=ALU.mult),
            reads=[b_thrs], writes=[b_thrs])
        em.op("dve", lambda e: e.tensor_scalar(
            out=wm[:], in0=ge[:], scalar1=gi[:], scalar2=thrs[:], op0=ALU.mult, op1=ALU.add),
            reads=[b_ge, b_gi, b_thrs], writes=[b_wm])
        em.op("dve", lambda e: e.tensor_scalar_max(wm[:], wm[:], 0.0), reads=[b_wm], writes=[b_wm])
        em.op("dve", lambda e: e.reduce_sum(out=gs_[:], in_=wm[:], axis=AX), reads=[b_wm], writes=[b_gs])
        em.op("dve", lambda e: e.tensor_scalar_max(gs_[:], gs_[:], 1e-30), reads=[b_gs], writes=[b_gs])
        em.op("dve", lambda e: e.reciprocal(out=gi[:], in_=gs_[:]), reads=[b_gs], writes=[b_gi])
        em.op("dve", lambda e: e.tensor_scalar(
            out=wm[:], in0=wm[:], scalar1=gi[:], scalar2=2.0, op0=ALU.mult, op1=ALU.mult),
            reads=[b_wm, b_gi], writes=[b_wm])
        for ee in range(8):
            em.op("dve", lambda e, ti=ti, ee=ee: e.tensor_scalar(
                out=h0s[:, ee * 16:(ee + 1) * 16], in0=h0[:, ti, ee * 16:(ee + 1) * 16],
                scalar1=wm[:, ee:ee + 1], scalar2=None, op0=ALU.mult),
                reads=[b_h0, b_wm], writes=[b_h0s])
        p.trans128(h0s[:], h0T[:, ti, :], b_h0s, b_h0T, ti)
    ar.free(gm, ge, gs_, gi, thrs, wm, h0s)
    em.barrier()
    if DBG.get("h0T"):
        p.tap("h0T", h0T[:], [128, NTILE * 128], BF16, [b_h0T])

    tabB2 = ar.alloc([128, 2, 2, FC, 512], BF16, "tabB2")
    b_tB2 = [Buf("tb20"), Buf("tb21")]
    bslice = ar.alloc([128, 2, 512], BF16, "bslice")
    b_bs = [Buf("bs0"), Buf("bs1")]
    xslice = ar.alloc([128, 2, 512], F32, "xslice")
    b_xs = [Buf("xs0"), Buf("xs1")]
    gsil = ar.alloc([128, 512], F32, "gsil")
    b_gsil = Buf("gsil")
    ot = ar.alloc([128, 2, 512], F32, "ot")
    b_ot = [Buf("ot0"), Buf("ot1")]
    for ob in range(8):
        tb, btb = tabB2[:, ob % 2], b_tB2[ob % 2]
        em.dma("pool", "dw" + str(2 * (ob % 2)), lambda e, ob=ob, tb=tb: e.dma_start(
            out=tb[:, 0], in_=p.d_cosB_t[ob].rearrange("p (c n) -> p c n", c=FC)),
            writes=[btb])
        em.dma("pool", "dw" + str(2 * (ob % 2) + 1), lambda e, ob=ob, tb=tb: e.dma_start(
            out=tb[:, 1], in_=p.d_sinB_t[ob].rearrange("p (c n) -> p c n", c=FC)),
            writes=[btb])
        for ti in range(NTILE):
            par = ti % 2
            bank_s = p.b_acc[par]
            psl_s = p.ps_acc[:, par * 512:(par + 1) * 512]
            bank_m = p.b_acc[2 + par]
            psl_m = p.ps_acc[:, (2 + par) * 512:(3 + par) * 512]
            bsl, bbs = bslice[:, par], b_bs[par]
            xsl, bxs = xslice[:, par], b_xs[par]
            osl, bot = ot[:, par], b_ot[par]
            em.dma("sp", "dsp" + str(par), lambda e, ob=ob, ti=ti, bsl=bsl: e.dma_start(
                out=bsl, in_=p.d_base0[ti * 128:(ti + 1) * 128, ob * 512:(ob + 1) * 512]),
                reads=[p.b_base0d], writes=[bbs])
            em.dma("sp", "dsp" + str(2 + par), lambda e, ob=ob, ti=ti, xsl=xsl: e.dma_start(
                out=xsl, in_=p.d_x[ti * 128:(ti + 1) * 128, ob * 512:(ob + 1) * 512]),
                writes=[bxs])
            mms = []
            for fcc in range(FC):
                mms.append(lambda e, fcc=fcc, ti=ti, tb=tb, psl_s=psl_s: e.matmul(
                    psl_s, Y[:, 0, fcc, ti * 128:(ti + 1) * 128], tb[:, 0, fcc, :],
                    start=(fcc == 0), stop=False))
                mms.append(lambda e, fcc=fcc, ti=ti, tb=tb, psl_s=psl_s: e.matmul(
                    psl_s, Y[:, 1, fcc, ti * 128:(ti + 1) * 128], tb[:, 1, fcc, :],
                    start=False, stop=(fcc == FC - 1)))
            em.mm_group(mms, reads=[b_Y, btb], writes=[bank_s])
            em.mm_group([lambda e, ob=ob, ti=ti, psl_m=psl_m: e.matmul(
                psl_m, h0T[:, ti, :], B0[:, ob * 512:(ob + 1) * 512], start=True, stop=True)],
                reads=[b_h0T, b_B0], writes=[bank_m])
            em.op("dve", lambda e, psl_m=psl_m, bsl=bsl: e.tensor_tensor(
                out=gsil[:], in0=psl_m, in1=bsl, op=ALU.add),
                reads=[bank_m, bbs], writes=[b_gsil])
            em.op("act", lambda e: e.activation(out=gsil[:], in_=gsil[:], func=AF.Silu, bias=p.zero1[:]),
                  reads=[b_gsil], writes=[b_gsil])
            em.op("dve", lambda e, psl_s=psl_s, osl=osl: e.tensor_tensor(
                out=osl, in0=psl_s, in1=gsil[:], op=ALU.mult),
                reads=[bank_s, b_gsil], writes=[bot])
            em.op("dve", lambda e, osl=osl, xsl=xsl: e.tensor_tensor(
                out=osl, in0=osl, in1=xsl, op=ALU.add),
                reads=[bxs, bot], writes=[bot])
            em.dma("sp", "dst" + str(par), lambda e, ob=ob, ti=ti, osl=osl: e.dma_start(
                out=p.d_x1[ti * 128:(ti + 1) * 128, ob * 512:(ob + 1) * 512], in_=osl),
                reads=[bot], writes=[p.b_x1d])
    ar.free(tabB2, bslice, xslice, gsil, ot, Y, B0, h0T)
    ar.free(h0, gate, thr, m8, i8, wn)
    ar.free(*[lnscratch[i] for i in (0, 2, 4, 6)])



def build_l1(p, standalone=False):
    nc, em, ar = p.nc, p.em, p.ar
    if standalone:
        p.d_x1 = nc.dram_tensor("x1_in", [NTOK, D], F32, kind="ExternalInput").ap()
        p.b_x1d = Buf("x1_in")
    d_x1, b_x1d = p.d_x1, p.b_x1d

    p.d_out = nc.dram_tensor("out", [NTOK, D], F32, kind="ExternalOutput").ap()
    p.b_outd = Buf("out")
    d_wq = nc.dram_tensor("w_qkv", [QKV_NBLK, 128, DC * 512], F32, kind="ExternalInput").ap()
    d_bq = nc.dram_tensor("b_qkv", [1, QKV_COLS], F32, kind="ExternalInput").ap()
    d_wo = nc.dram_tensor("w_wo", [8, 128, DC * 512], F32, kind="ExternalInput").ap()
    d_bo = nc.dram_tensor("b_wo", [1, D], F32, kind="ExternalInput").ap()
    d_w1 = nc.dram_tensor("w_w1", [8, 128, DC * 512], F32, kind="ExternalInput").ap()
    d_b1 = nc.dram_tensor("b_w1", [1, D], F32, kind="ExternalInput").ap()
    d_B1 = nc.dram_tensor("w_B1", [128, D], F32, kind="ExternalInput").ap()
    d_lw = nc.dram_tensor("lora_w", [1, 8], F32, kind="ExternalInput").ap()
    d_rc = nc.dram_tensor("rope_cos", [NTOK, HEADS * 64], BF16, kind="ExternalInput").ap()
    d_rs = nc.dram_tensor("rope_sin", [NTOK, HEADS * 64], BF16, kind="ExternalInput").ap()
    d_rck = nc.dram_tensor("rope_cos_k", [NTOK, 512], BF16, kind="ExternalInput").ap()
    d_rsk = nc.dram_tensor("rope_sin_k", [NTOK, 512], BF16, kind="ExternalInput").ap()
    d_mask2 = nc.dram_tensor("mask2", [128, 256], F32, kind="ExternalInput").ap()
    d_xn1T = nc.dram_tensor("xn1T_d", [NTILE, 128, DC * 128], BF16, kind="Internal").ap()
    b_xn1Td = Buf("xn1T_d")
    d_qT = nc.dram_tensor("qT_d", [NTILE, 128, HEADS * 128], BF16, kind="Internal").ap()
    b_qTd = Buf("qT_d")
    d_qkv = nc.dram_tensor("qkv_d", [NTOK, QKVD], BF16, kind="Internal").ap()
    b_qkvd = Buf("qkv_d")
    kvkind = "ExternalOutput" if DBG.get("kv") else "Internal"
    d_ccin = nc.dram_tensor("cc_in", [NTOK, 2 * KVDIM], BF16, kind=kvkind).ap()
    b_ccin = Buf("cc_in")
    d_ccout = nc.dram_tensor("cc_out", [2 * NTOK, 2 * KVDIM], BF16, kind=kvkind).ap()
    b_ccout = Buf("cc_out")

    h1T = ar.alloc([128, NTILE, 128], BF16, "h1T")
    b_h1T = Buf("h1T")
    h1res = ar.alloc([128, NTILE, 128], F32, "h1res")
    b_h1res = Buf("h1res")
    masks = ar.alloc([128, 256], F32, "masks")
    b_masks = Buf("masks")
    em.dma("sp", "dsp0", lambda e: e.dma_start(out=masks[:], in_=d_mask2), writes=[b_masks])
    coefb = ar.alloc([128, 8], F32, "coefb")
    b_coefb = Buf("coefb")
    lwrow = ar.alloc([1, 8], BF16, "lwrow")
    b_lwrow = Buf("lwrow")
    em.dma("pool", "dw3", lambda e: e.dma_start(out=lwrow[:], in_=d_lw), writes=[b_lwrow])
    em.mm_group([lambda e: e.matmul(p.ps_b5[:, 0:8], p.ones1[:], lwrow[:], start=True, stop=True)],
                reads=[p.b_ones, b_lwrow], writes=[p.b_b5])
    em.op("dve", lambda e: e.tensor_copy(out=coefb[:], in_=p.ps_b5[:, 0:8]),
          reads=[p.b_b5], writes=[b_coefb])

    lnscr = (ar.alloc([128, 48], F32, "stats1"), Buf("stats1"),
             ar.alloc([128, 2], F32, "mv1"), Buf("mv1"),
             ar.alloc([128, 1], F32, "rstd1"), Buf("rstd1"),
             ar.alloc([128, 1], F32, "nbias1"), Buf("nbias1"))
    xn1T = ar.alloc([128, DC, NTOK], BF16, "xn1Tq")
    b_xn1T = Buf("xn1Tq")
    xa = ar.alloc([128, 2, D], F32, "x1tile")
    b_xa = [Buf("x1a0"), Buf("x1a1")]
    xn1 = ar.alloc([128, D], BF16, "xn1")
    b_xn1 = Buf("xn1")
    for ti in range(NTILE):
        xab, b_x = xa[:, ti % 2, :], b_xa[ti % 2]
        em.dma("sp", "dsp" + str(ti % 2), lambda e, ti=ti, xab=xab: e.dma_start(
            out=xab, in_=d_x1[ti * 128:(ti + 1) * 128, :]), reads=[b_x1d], writes=[b_x])
        layernorm_tile(p, xab, b_x, xn1[:], b_xn1, lnscr)
        for dc in range(DC):
            p.trans128(xn1[:, dc * 128:(dc + 1) * 128], xn1T[:, dc, ti * 128:(ti + 1) * 128],
                       b_xn1, b_xn1T, dc)
        em.dma("sp", "dst0", lambda e, ti=ti: e.dma_start(
            out=d_xn1T[ti].rearrange("p (c n) -> p c n", c=DC),
            in_=xn1T[:, :, ti * 128:(ti + 1) * 128]),
            reads=[b_xn1T], writes=[b_xn1Td])
    ar.free(xa, xn1)
    em.barrier()
    wq = ar.alloc([128, 2, DC, 512], BF16, "wqblk")
    b_wq = [Buf("wq0"), Buf("wq1")]
    brow = ar.alloc([1, QKV_COLS], BF16, "brow1")
    b_brow = Buf("brow1")
    evq = ar.alloc([128, 2, 512], BF16, "evq")
    b_evq = [Buf("evq0"), Buf("evq1")]
    em.dma("pool", "dw2", lambda e: e.dma_start(out=brow[:], in_=d_bq), writes=[b_brow])
    for blk in range(QKV_NBLK):
        wb, bw = wq[:, blk % 2], b_wq[blk % 2]
        em.dma("pool", "dw" + str(blk % 2), lambda e, blk=blk, wb=wb: e.dma_start(
            out=wb, in_=d_wq[blk].rearrange("p (c n) -> p c n", c=DC)),
            writes=[bw])
        for ti in range(NTILE):
            rb4 = (blk * NTILE + ti) % 4
            bank = p.b_acc[rb4]
            psl = p.ps_acc[:, rb4 * 512:(rb4 + 1) * 512]
            mms = []
            for kc in range(DC):
                mms.append(lambda e, kc=kc, ti=ti, wb=wb, psl=psl: e.matmul(
                    psl, xn1T[:, kc, ti * 128:(ti + 1) * 128], wb[:, kc, :],
                    start=(kc == 0), stop=False))
            mms.append(lambda e, blk=blk, psl=psl: e.matmul(
                psl, p.ones1[:], brow[:, blk * 512:(blk + 1) * 512], start=False, stop=True))
            em.mm_group(mms, reads=[b_xn1T, bw, p.b_ones, b_brow], writes=[bank])
            if blk < 12:
                ev, bev = evq[:, ti % 2, :], b_evq[ti % 2]
                em.op("act", lambda e, psl=psl, ev=ev: e.activation(
                    out=ev, in_=psl, func=AF.Identity, bias=p.zero1[:]),
                    reads=[bank], writes=[bev])
                em.dma("sp", "dst" + str(ti % 2), lambda e, blk=blk, ti=ti, ev=ev: e.dma_start(
                    out=d_qkv[ti * 128:(ti + 1) * 128, blk * 512:(blk + 1) * 512], in_=ev),
                    reads=[bev], writes=[b_qkvd])
            else:
                em.op("act", lambda e, ti=ti, psl=psl: e.activation(
                    out=h1res[:, ti, :], in_=psl[:, 0:128], func=AF.Identity, bias=p.zero1[:]),
                    reads=[bank], writes=[b_h1res])
    ar.free(wq, brow, evq, xn1T)
    ar.free(*[lnscr[i] for i in (0, 2, 4, 6)])
    em.barrier()

    qkvt = ar.alloc([128, 2, QKVD], BF16, "qkvt")
    b_qkvt = [Buf("qkvt0"), Buf("qkvt1")]
    crep = ar.alloc([128, 2048], BF16, "crep")
    srep = ar.alloc([128, 2048], BF16, "srep")
    b_crep, b_srep = Buf("crep"), Buf("srep")
    crepk = ar.alloc([128, 512], BF16, "crepk")
    srepk = ar.alloc([128, 512], BF16, "srepk")
    b_crepk, b_srepk = Buf("crepk"), Buf("srepk")
    qrot = ar.alloc([128, QDIM], BF16, "qrot")
    b_qrot = Buf("qrot")
    krot = ar.alloc([128, KVDIM], BF16, "krot")
    b_krot = Buf("krot")
    tmpA = ar.alloc([128, 2048], F32, "tmpA")
    tmpB = ar.alloc([128, 2048], F32, "tmpB")
    b_tA, b_tB = Buf("tmpA"), Buf("tmpB")
    stg = ar.alloc([128, HEADS, 128], BF16, "stg")
    b_stg = Buf("stg")
    h1s = ar.alloc([128, 128], BF16, "h1s")
    b_h1s = Buf("h1s")
    for ti in range(NTILE):
        qk, bqk = qkvt[:, ti % 2, :], b_qkvt[ti % 2]
        em.dma("sp", "dsp" + str(ti % 2), lambda e, ti=ti, qk=qk: e.dma_start(
            out=qk, in_=d_qkv[ti * 128:(ti + 1) * 128, :]),
            reads=[b_qkvd], writes=[bqk])
        em.dma("sp", "dsp2", lambda e, ti=ti: e.dma_start(
            out=crep[:], in_=d_rc[ti * 128:(ti + 1) * 128, 0:2048]), writes=[b_crep])
        em.dma("sp", "dsp3", lambda e, ti=ti: e.dma_start(
            out=srep[:], in_=d_rs[ti * 128:(ti + 1) * 128, 0:2048]), writes=[b_srep])
        em.dma("sp", "dsp2", lambda e, ti=ti: e.dma_start(
            out=crepk[:], in_=d_rck[ti * 128:(ti + 1) * 128, :]), writes=[b_crepk])
        em.dma("sp", "dsp3", lambda e, ti=ti: e.dma_start(
            out=srepk[:], in_=d_rsk[ti * 128:(ti + 1) * 128, :]), writes=[b_srepk])
        em.op("dve", lambda e, qk=qk: e.tensor_tensor(out=tmpA[:], in0=qk[0:128, 0:QDIM:2], in1=crep[:], op=ALU.mult),
              reads=[bqk, b_crep], writes=[b_tA])
        em.op("dve", lambda e, qk=qk: e.tensor_tensor(out=tmpB[:], in0=qk[0:128, 1:QDIM:2], in1=srep[:], op=ALU.mult),
              reads=[bqk, b_srep], writes=[b_tB])
        em.op("dve", lambda e: e.tensor_tensor(out=qrot[:, 0:QDIM:2], in0=tmpA[:], in1=tmpB[:], op=ALU.subtract),
              reads=[b_tA, b_tB], writes=[b_qrot])
        em.op("dve", lambda e, qk=qk: e.tensor_tensor(out=tmpA[:], in0=qk[0:128, 0:QDIM:2], in1=srep[:], op=ALU.mult),
              reads=[bqk, b_srep], writes=[b_tA])
        em.op("dve", lambda e, qk=qk: e.tensor_tensor(out=tmpB[:], in0=qk[0:128, 1:QDIM:2], in1=crep[:], op=ALU.mult),
              reads=[bqk, b_crep], writes=[b_tB])
        em.op("dve", lambda e: e.tensor_tensor(out=qrot[:, 1:QDIM:2], in0=tmpA[:], in1=tmpB[:], op=ALU.add),
              reads=[b_tA, b_tB], writes=[b_qrot])
        em.op("dve", lambda e, qk=qk: e.tensor_tensor(
            out=tmpA[:, 0:512], in0=qk[0:128, QDIM:QDIM + KVDIM:2], in1=crepk[:], op=ALU.mult),
            reads=[bqk, b_crepk], writes=[b_tA])
        em.op("dve", lambda e, qk=qk: e.tensor_tensor(
            out=tmpB[:, 0:512], in0=qk[0:128, QDIM + 1:QDIM + KVDIM:2], in1=srepk[:], op=ALU.mult),
            reads=[bqk, b_srepk], writes=[b_tB])
        em.op("dve", lambda e: e.tensor_tensor(
            out=krot[:, 0:KVDIM:2], in0=tmpA[:, 0:512], in1=tmpB[:, 0:512], op=ALU.subtract),
            reads=[b_tA, b_tB], writes=[b_krot])
        em.op("dve", lambda e, qk=qk: e.tensor_tensor(
            out=tmpA[:, 0:512], in0=qk[0:128, QDIM:QDIM + KVDIM:2], in1=srepk[:], op=ALU.mult),
            reads=[bqk, b_srepk], writes=[b_tA])
        em.op("dve", lambda e, qk=qk: e.tensor_tensor(
            out=tmpB[:, 0:512], in0=qk[0:128, QDIM + 1:QDIM + KVDIM:2], in1=crepk[:], op=ALU.mult),
            reads=[bqk, b_crepk], writes=[b_tB])
        em.op("dve", lambda e: e.tensor_tensor(
            out=krot[:, 1:KVDIM:2], in0=tmpA[:, 0:512], in1=tmpB[:, 0:512], op=ALU.add),
            reads=[b_tA, b_tB], writes=[b_krot])
        em.dma("sp", "dst1", lambda e, ti=ti: e.dma_start(
            out=d_ccin[ti * 128:(ti + 1) * 128, 0:KVDIM], in_=krot[:]),
            reads=[b_krot], writes=[b_ccin])
        em.dma("sp", "dst1", lambda e, ti=ti, qk=qk: e.dma_start(
            out=d_ccin[ti * 128:(ti + 1) * 128, KVDIM:2 * KVDIM], in_=qk[0:128, QDIM + KVDIM:QKVD]),
            reads=[bqk], writes=[b_ccin])
        for ee in range(8):
            em.op("dve", lambda e, ti=ti, ee=ee: e.tensor_scalar(
                out=h1s[:, ee * 16:(ee + 1) * 16], in0=h1res[:, ti, ee * 16:(ee + 1) * 16],
                scalar1=coefb[:, ee:ee + 1], scalar2=2.0, op0=ALU.mult, op1=ALU.mult),
                reads=[b_h1res, b_coefb], writes=[b_h1s])
        p.trans128(h1s[:], h1T[:, ti, :], b_h1s, b_h1T, ti)
        for hh in range(HEADS):
            p.trans128(qrot[:, hh * 128:(hh + 1) * 128], stg[:, hh, :], b_qrot, b_stg, hh)
        em.dma("sp", "dst0", lambda e, ti=ti: e.dma_start(
            out=d_qT[ti].rearrange("p (c n) -> p c n", c=HEADS), in_=stg[:]),
            reads=[b_stg], writes=[b_qTd])
        if DBG.get("qkv") and ti == 0:
            p.tap("qkv", qk, [128, QKVD], BF16, [bqk], force=True)
            p.tap("qrot", qrot[:], [128, QDIM], BF16, [b_qrot], force=True)
    ar.free(qkvt, crep, srep, crepk, srepk, qrot, krot, tmpA, tmpB, stg, h1s, h1res)
    em.barrier()

    em.op("pool", lambda e: e.collective_compute(
        "AllGather", ALU.bypass,
        replica_groups=[[0, 1], [2, 3], [4, 5], [6, 7]],
        ins=[d_ccin.opt()], outs=[d_ccout.opt()]),
        reads=[b_ccin], writes=[b_ccout])
    KT = ar.alloc([128, KVH, S], BF16, "KT")
    b_KT = Buf("KTfull")
    Vsb = ar.alloc([128, 16, KVDIM], BF16, "Vsb")
    b_V = Buf("Vsb")
    ktmp = ar.alloc([128, 2, KVDIM], BF16, "ktmp")
    b_kt = [Buf("kt0"), Buf("kt1")]
    for rp in range(2):
        for tp in range(NTILE):
            ci = 2 * tp + rp
            row0 = rp * NTOK + tp * 128
            kb, bkb = ktmp[:, ci % 2, :], b_kt[ci % 2]
            em.dma("sp", "dsp" + str(ci % 2), lambda e, row0=row0, kb=kb: e.dma_start(
                out=kb, in_=d_ccout[row0:row0 + 128, 0:KVDIM]),
                reads=[b_ccout], writes=[bkb])
            for kv in range(KVH):
                p.trans128(kb[:, kv * 128:(kv + 1) * 128], KT[:, kv, ci * 128:(ci + 1) * 128],
                           bkb, b_KT, kv)
            em.dma("sp", "dsp" + str(2 + ci % 2), lambda e, row0=row0, ci=ci: e.dma_start(
                out=Vsb[:, ci, :], in_=d_ccout[row0:row0 + 128, KVDIM:2 * KVDIM]),
                reads=[b_ccout], writes=[b_V])
    ar.free(ktmp)
    em.barrier()

    s1T = ar.alloc([128, HEADS, NTOK], BF16, "s1T")
    b_s1T = Buf("s1T")
    qt = ar.alloc([128, 2, HEADS, 128], BF16, "qt")
    b_qt = [Buf("qt0"), Buf("qt1")]
    mask2b = ar.alloc([128, 256], BF16, "mask2b")
    b_mask2b = Buf("mask2b")
    em.op("dve", lambda e: e.tensor_copy(out=mask2b[:], in_=masks[:]), reads=[b_masks], writes=[b_mask2b])
    probs = ar.alloc([128, 4, 2, 512], BF16, "probs")
    b_pr = [[Buf(f"pr{s}{i}") for i in range(2)] for s in range(4)]
    PT = ar.alloc([128, 4, 2, 128], BF16, "PT")
    b_PT = [[Buf(f"PT{s}{i}") for i in range(2)] for s in range(4)]
    negm = ar.alloc([128, 4, 2], F32, "negm")
    b_negm = [Buf(f"negm{s}") for s in range(4)]
    lrow = ar.alloc([128, 4], F32, "lrow")
    b_lr = [Buf(f"lr{s}") for s in range(4)]
    acc = ar.alloc([128, 4, 128], F32, "acch")
    b_acc_h = [Buf(f"acc{s}") for s in range(4)]
    cm = ar.alloc([128, 4, 2], F32, "cm")
    b_cm = [[Buf(f"cm{s}{i}") for i in range(2)] for s in range(4)]
    alph = ar.alloc([128, 4, 2], F32, "alph")
    b_al = [[Buf(f"al{s}{i}") for i in range(2)] for s in range(4)]
    dmm = ar.alloc([128, 4, 2], F32, "dmm")
    b_dm = [[Buf(f"dm{s}{i}") for i in range(2)] for s in range(4)]
    lc = ar.alloc([128, 4, 2], F32, "lc")
    b_lc = [[Buf(f"lc{s}{i}") for i in range(2)] for s in range(4)]
    linv = ar.alloc([128, 4], F32, "linv")
    b_li = [Buf(f"li{s}") for s in range(4)]
    sv = ar.alloc([128, 4, 128], BF16, "sv")
    b_sv = [Buf(f"sv{s}") for s in range(4)]

    def head_pipeline(ti, hh, s, qb, bqb):
        nk = (ti + 1) * 256
        nb = (nk + 511) // 512
        kv = hh // GRP
        bng, blr, bah = b_negm[s], b_lr[s], b_acc_h[s]
        ngl = negm[:, s, :]
        bank = p.b_acc[s]
        pvps = p.ps_acc[:, s * 512:s * 512 + 512]
        bpv = bank
        cur, new = 0, 1
        for c2 in range(nb):
            w = min(512, nk - 512 * c2)
            ci = c2 % 2
            psl = p.ps_acc[:, s * 512:s * 512 + w]
            prb, bprb = probs[:, s, ci, :], b_pr[s][ci]
            cml, bcml = cm[:, s, ci:ci + 1], b_cm[s][ci]
            dml = dmm[:, s, ci:ci + 1]
            all_, bal = alph[:, s, ci:ci + 1], b_al[s][ci]
            lcl, blcl = lc[:, s, ci:ci + 1], b_lc[s][ci]
            first, last = (c2 == 0), (c2 == nb - 1)
            mms = [lambda e, hh=hh, kv=kv, w=w, c2=c2, psl=psl, qb=qb, last=last: e.matmul(
                psl, qb[:, hh, :], KT[:, kv, c2 * 512:c2 * 512 + w],
                start=True, stop=not last)]
            if last:
                mms.append(lambda e, psl=psl, w=w: e.matmul(
                    psl[:, w - 256:w], p.ident[:], mask2b[:], start=False, stop=True))
            em.mm_group(mms, reads=[bqb, b_KT, p.b_ident, b_mask2b], writes=[bank])
            yield
            em.op("dve", lambda e, psl=psl, cml=cml: e.reduce_max(
                out=cml, in_=psl, axis=AX, negate=True), reads=[bank], writes=[bcml])
            if first:
                em.op("dve", lambda e, ngl=ngl, cml=cml, new=new: e.tensor_copy(
                    out=ngl[:, new:new + 1], in_=cml), reads=[bcml], writes=[bng])
            else:
                em.op("dve", lambda e, ngl=ngl, cml=cml, cur=cur, new=new: e.tensor_tensor(
                    out=ngl[:, new:new + 1], in0=ngl[:, cur:cur + 1], in1=cml, op=ALU.min),
                    reads=[bcml, bng], writes=[bng])
                em.op("dve", lambda e, ngl=ngl, dml=dml, cur=cur, new=new: e.tensor_tensor(
                    out=dml, in0=ngl[:, new:new + 1], in1=ngl[:, cur:cur + 1], op=ALU.subtract),
                    reads=[bng], writes=[b_dm[s][ci]])
            yield
            if not first:
                em.op("act", lambda e, all_=all_, dml=dml: e.activation(
                    out=all_, in_=dml, func=AF.Exp, bias=p.zero1[:]),
                    reads=[b_dm[s][ci], p.b_zero1], writes=[bal])
            em.op("act", lambda e, psl=psl, prb=prb, w=w, ngl=ngl, new=new, lcl=lcl: e.activation(
                out=prb[0:128, 0:w], in_=psl, func=AF.Exp, bias=ngl[:, new:new + 1],
                accum_out=lcl),
                reads=[bank, bng], writes=[bprb, blcl])
            yield
            if first:
                em.op("dve", lambda e, lcl=lcl: e.tensor_copy(
                    out=lrow[:, s:s + 1], in_=lcl), reads=[blcl], writes=[blr])
            else:
                em.op("dve", lambda e, all_=all_, lcl=lcl: e.scalar_tensor_tensor(
                    out=lrow[:, s:s + 1], in0=lrow[:, s:s + 1], scalar=all_, in1=lcl,
                    op0=ALU.mult, op1=ALU.add),
                    reads=[bal, blcl, blr], writes=[blr])
            nch = (w + 127) // 128
            for kc in range(nch):
                pts, bpts = PT[:, s, kc % 2, :], b_PT[s][kc % 2]
                p.trans128(prb[0:128, kc * 128:(kc + 1) * 128], pts, bprb, bpts, s,
                           copy_eng="act" if kc % 2 else "dve")
                em.mm_group([lambda e, kc=kc, kv=kv, c2=c2, pts=pts, nch=nch, pvps=pvps: e.matmul(
                    pvps[:, 0:128], pts, Vsb[:, 4 * c2 + kc, kv * 128:(kv + 1) * 128],
                    start=(kc == 0), stop=(kc == nch - 1), skip_group_check=True)],
                    reads=[bpts, b_V], writes=[bpv])
                if kc % 2 == 1 or kc == nch - 1:
                    yield
            if first:
                em.op("dve", lambda e, pvps=pvps: e.tensor_copy(
                    out=acc[:, s, :], in_=pvps[:, 0:128]),
                    reads=[bpv], writes=[bah])
            else:
                em.op("dve", lambda e, all_=all_, pvps=pvps: e.scalar_tensor_tensor(
                    out=acc[:, s, :], in0=acc[:, s, :], scalar=all_, in1=pvps[:, 0:128],
                    op0=ALU.mult, op1=ALU.add),
                    reads=[bal, bpv, bah], writes=[bah])
            cur, new = new, cur
            yield
        em.op("dve", lambda e: e.reciprocal(out=linv[:, s:s + 1], in_=lrow[:, s:s + 1]),
              reads=[blr], writes=[b_li[s]])
        em.op("act", lambda e: e.activation(
            out=sv[:, s, :], in_=acc[:, s, :], func=AF.Identity, scale=linv[:, s:s + 1],
            bias=p.zero1[:]),
            reads=[bah, b_li[s], p.b_zero1], writes=[b_sv[s]])
        yield
        p.trans128(sv[:, s, :], s1T[:, hh, ti * 128:(ti + 1) * 128], b_sv[s], b_s1T, s)
        yield

    for ti in range(NTILE):
        qb, bqb = qt[:, ti % 2], b_qt[ti % 2]
        em.dma("sp", "dsp" + str(ti % 2), lambda e, ti=ti, qb=qb: e.dma_start(
            out=qb, in_=d_qT[ti].rearrange("p (c n) -> p c n", c=HEADS)),
            reads=[b_qTd], writes=[bqb])
        for h0 in range(0, HEADS, 4):
            live = [head_pipeline(ti, h0 + s, s, qb, bqb) for s in range(4)]
            for s in range(4):
                for _ in range(3 * (3 - s)):
                    try:
                        next(live[s])
                    except StopIteration:
                        break
            while live:
                nxt = []
                for g in live:
                    try:
                        next(g)
                        nxt.append(g)
                    except StopIteration:
                        pass
                live = nxt
    ar.free(qt, probs, PT, negm, lrow, acc, cm, alph, dmm, lc, linv, sv, KT, Vsb, masks, mask2b)
    if DBG.get("s1T"):
        p.tap("s1T", s1T[:], [128, HEADS * NTOK], BF16, [b_s1T])
    em.barrier()

    xn1Ts = ar.alloc([128, DC, NTOK], BF16, "xn1Ts")
    b_xn1Ts = Buf("xn1Ts")
    for ti in range(NTILE):
        em.dma("sp", "dsp" + str(ti % 2), lambda e, ti=ti: e.dma_start(
            out=xn1Ts[:, :, ti * 128:(ti + 1) * 128], in_=d_xn1T[ti].rearrange("p (c n) -> p c n", c=DC)),
            reads=[b_xn1Td], writes=[b_xn1Ts])
    wblk = ar.alloc([128, DC, 512], BF16, "wblk1")
    b_wb = Buf("wblk1")
    brow_o = ar.alloc([1, 512], BF16, "brow_o")
    b_bro = Buf("brow_o")
    brow_1 = ar.alloc([1, 512], BF16, "brow_1")
    b_br1 = Buf("brow_1")
    B1sb = ar.alloc([128, D], BF16, "B1sb")
    b_B1 = Buf("B1sb")
    proj_sb = ar.alloc([128, NTILE, 512], BF16, "proj_sb")
    b_proj = Buf("proj_sb")
    gsil = ar.alloc([128, 512], F32, "gsil1")
    b_gs = Buf("gsil1")
    xsl = ar.alloc([128, 2, 512], F32, "xsl1")
    b_xsl = [Buf("xsl0"), Buf("xsl1")]
    osl = ar.alloc([128, 2, 512], F32, "osl1")
    b_osl = [Buf("osl0"), Buf("osl1")]
    em.dma("pool", "dw3", lambda e: e.dma_start(out=B1sb[:], in_=d_B1), writes=[b_B1])
    for ob in range(8):
        em.dma("pool", "dw2", lambda e, ob=ob: e.dma_start(
            out=brow_o[:], in_=d_bo[:, ob * 512:(ob + 1) * 512]), writes=[b_bro])
        em.dma("pool", "dw2", lambda e, ob=ob: e.dma_start(
            out=brow_1[:], in_=d_b1[:, ob * 512:(ob + 1) * 512]), writes=[b_br1])
        em.dma("pool", "dw0", lambda e, ob=ob: e.dma_start(
            out=wblk[:], in_=d_wo[ob].rearrange("p (c n) -> p c n", c=DC)),
            writes=[b_wb])
        for ti in range(NTILE):
            rb4 = (ob * NTILE + ti) % 4
            bank = p.b_acc[rb4]
            psl = p.ps_acc[:, rb4 * 512:(rb4 + 1) * 512]
            mms = []
            for kc in range(DC):
                mms.append(lambda e, kc=kc, ti=ti, psl=psl: e.matmul(
                    psl, s1T[:, kc, ti * 128:(ti + 1) * 128], wblk[:, kc, :],
                    start=(kc == 0), stop=False))
            mms.append(lambda e, ob=ob, psl=psl: e.matmul(
                psl, p.ones1[:], brow_o[:], start=False, stop=True))
            em.mm_group(mms, reads=[b_s1T, b_wb, p.b_ones, b_bro], writes=[bank])
            em.op("act", lambda e, ti=ti, psl=psl: e.activation(
                out=proj_sb[:, ti, :], in_=psl, func=AF.Identity, bias=p.zero1[:]),
                reads=[bank], writes=[b_proj])
        em.dma("pool", "dw1", lambda e, ob=ob: e.dma_start(
            out=wblk[:], in_=d_w1[ob].rearrange("p (c n) -> p c n", c=DC)),
            writes=[b_wb])
        for ti in range(NTILE):
            par = ti % 2
            bank = p.b_acc[2 + par]
            psl = p.ps_acc[:, (2 + par) * 512:(3 + par) * 512]
            xs_, bxs = xsl[:, par], b_xsl[par]
            os_, bos = osl[:, par], b_osl[par]
            em.dma("sp", "dsp" + str(2 + par), lambda e, ob=ob, ti=ti, xs_=xs_: e.dma_start(
                out=xs_, in_=d_x1[ti * 128:(ti + 1) * 128, ob * 512:(ob + 1) * 512]),
                reads=[b_x1d], writes=[bxs])
            mms = []
            for kc in range(DC):
                mms.append(lambda e, kc=kc, ti=ti, psl=psl: e.matmul(
                    psl, xn1Ts[:, kc, ti * 128:(ti + 1) * 128], wblk[:, kc, :],
                    start=(kc == 0), stop=False))
            mms.append(lambda e, ob=ob, psl=psl: e.matmul(
                psl, p.ones1[:], brow_1[:], start=False, stop=False))
            mms.append(lambda e, ob=ob, ti=ti, psl=psl: e.matmul(
                psl, h1T[:, ti, :], B1sb[:, ob * 512:(ob + 1) * 512], start=False, stop=True))
            em.mm_group(mms, reads=[b_xn1Ts, b_wb, p.b_ones, b_br1, b_h1T, b_B1], writes=[bank])
            em.op("act", lambda e, psl=psl: e.activation(
                out=gsil[:], in_=psl, func=AF.Silu, bias=p.zero1[:]),
                reads=[bank], writes=[b_gs])
            em.op("dve", lambda e, ti=ti, os_=os_: e.tensor_tensor(
                out=os_, in0=proj_sb[:, ti, :], in1=gsil[:], op=ALU.mult),
                reads=[b_proj, b_gs], writes=[bos])
            em.op("dve", lambda e, os_=os_, xs_=xs_: e.tensor_tensor(
                out=os_, in0=os_, in1=xs_, op=ALU.add),
                reads=[bxs, bos], writes=[bos])
            em.dma("sp", "dst" + str(par), lambda e, ob=ob, ti=ti, os_=os_: e.dma_start(
                out=p.d_out[ti * 128:(ti + 1) * 128, ob * 512:(ob + 1) * 512], in_=os_),
                reads=[bos], writes=[p.b_outd])
    ar.free(xn1Ts, wblk, brow_o, brow_1, B1sb, proj_sb, gsil, xsl, osl, s1T, h1T, coefb, lwrow)


def finalize(p):
    p.em.barrier()


def build_nc(core, layers=(0,)):
    nc = bass.Bass("TRN2", target_bir_lowering=False, debug=False)
    es = ExitStack()
    with es:
        p = Prog(nc, es, core)
        p.prelude()
        if 0 in layers:
            build_l0(p)
        if 1 in layers:
            build_l1(p, standalone=(0 not in layers))
        finalize(p)
        block = es.enter_context(nc.Block())
        p.em.replay(block)
    return nc


def core_in_maps(inputs, core, layers=(0, 1), x1_shards=None):
    t = host_tables()
    w = host_weights(inputs)
    s, tiles = shard_rows(core)
    rows = np.concatenate([np.arange(tt * 128, (tt + 1) * 128) for tt in tiles])
    im = {"ident": t["ident"], "identf": t["ident"].astype(np.float32)}
    if 0 in layers:
        im.update({
            "x_in": shard_x(inputs["L0_attn_x"], core),
            "cosA": t["cosA"], "sinA": t["sinA"],
            "cosA_t": t["cosA_t"], "sinA_t": t["sinA_t"],
            "cosB_t": t["cosB_t"], "sinB_t": t["sinB_t"],
            "w_b2": w["w_b2"], "b_b2": w["b_b2"], "w_B0": w["w_B0"],
        })
    if 1 in layers:
        parity = core % 2
        q = np.arange(128)[:, None]
        k2 = np.arange(256)[None, :]
        mask2 = np.where(k2 <= q + 128 * parity, 0.0, MAXNEG).astype(np.float32)
        rc = t["rope_cos"][rows].astype(np.float32)
        rs = t["rope_sin"][rows].astype(np.float32)
        rcq = (rc * ATT_SCALE).astype(BF)
        rsq = (rs * ATT_SCALE).astype(BF)
        im.update({
            "rope_cos": np.ascontiguousarray(rcq),
            "rope_sin": np.ascontiguousarray(rsq),
            "rope_cos_k": np.ascontiguousarray(rc[:, 0:512].astype(BF)),
            "rope_sin_k": np.ascontiguousarray(rs[:, 0:512].astype(BF)),
            "mask2": mask2,
            "w_qkv": w["w_qkv"], "b_qkv": w["b_qkv"],
            "w_wo": w["w_wo"], "b_wo": w["b_wo"],
            "w_w1": w["w_w1"], "b_w1": w["b_w1"],
            "w_B1": w["w_B1"], "lora_w": w["lora_w"],
        })
        if 0 not in layers:
            im["x1_in"] = x1_shards[core]
    return im


FUSED = True
_NC_CACHE = {}


def _get_nc(layers):
    key = tuple(layers)
    if key not in _NC_CACHE:
        _NC_CACHE[key] = build_nc(0, layers=key)
    return _NC_CACHE[key]


def kernel(**inputs):
    if FUSED:
        nc = _get_nc((0, 1))
        im = [core_in_maps(inputs, c, (0, 1)) for c in range(NCORE)]
        res = run_bass_kernel_spmd(nc, im, core_ids=list(range(NCORE)))
        return unshard([res.results[c]["out"] for c in range(NCORE)])
    global DBG
    old = dict(DBG)
    DBG.clear()
    DBG["l0_out"] = True
    nc0 = _get_nc((0,))
    DBG.clear()
    DBG.update(old)
    im0 = [core_in_maps(inputs, c, (0,)) for c in range(NCORE)]
    r0 = run_bass_kernel_spmd(nc0, im0, core_ids=list(range(NCORE)))
    x1_shards = [np.asarray(r0.results[c]["x1_d"]) for c in range(NCORE)]
    nc1 = _get_nc((1,))
    im1 = [core_in_maps(inputs, c, (1,), x1_shards=x1_shards) for c in range(NCORE)]
    r1 = run_bass_kernel_spmd(nc1, im1, core_ids=list(range(NCORE)))
    return unshard([r1.results[c]["out"] for c in range(NCORE)])


# version 19 (final)
# speedup vs baseline: 1.2390x; 1.2390x over previous
import math
from contextlib import ExitStack

import numpy as np
import ml_dtypes

import concourse.bass as bass
import concourse.mybir as mybir
from concourse.bass_core_rust import IndirectOffsetOnAxis
from concourse.bass_utils import run_bass_kernel_spmd

F32 = mybir.dt.float32
BF16 = mybir.dt.bfloat16
U32 = mybir.dt.uint32
AF = mybir.ActivationFunctionType
ALU = mybir.AluOpType
AX = mybir.AxisListType.X
BF = ml_dtypes.bfloat16

Bsz, S, D = 4, 2048, 4096
NCORE = 8
NTILE, TOK = 8, 128
NTOK = NTILE * TOK
L = D
NF = L // 2 + 1
FC = 17
NFP = FC * 128
DC = D // 128
HEADS, KVH, HD = 32, 8, 128
GRP = HEADS // KVH
QDIM, KVDIM = HEADS * HD, KVH * HD
QKVD = QDIM + 2 * KVDIM
ROPE_THETA = 10000.0
ATT_SCALE = 1.0 / math.sqrt(HD)
MAXNEG = -1e20

B2_COLS, B2_NBLK = 4608, 9
OFF_H0, OFF_GATE, OFF_THR = 4096, 4224, 4232
QKV_COLS, QKV_NBLK = 6656, 13
OFF_H1 = 6144

DBG = {}


class Buf:
    __slots__ = ("name", "writer", "readers", "excl")

    def __init__(self, name, excl=False):
        self.name = name
        self.writer = None
        self.readers = []
        self.excl = excl


class Em:
    QUEUES = ("pe", "dve", "act", "sp", "pool")

    def __init__(self, nc, es):
        self.nc = nc
        self.tape = []
        self.sem = {}
        self.cnt = {}
        self.waited = {}
        for q in self.QUEUES:
            self.sem[q] = es.enter_context(nc.semaphore("q_" + q))
            self.cnt[q] = 0
        for d in ("dsp0", "dsp1", "dsp2", "dsp3", "dw0", "dw1", "dw2", "dw3",
                  "dg0", "dg1", "dg2", "dg3", "dst0", "dst1", "dcc"):
            self.sem[d] = es.enter_context(nc.semaphore("s_" + d))
            self.cnt[d] = 0

    def _emit_wait(self, q, semname, val):
        if val is None or val <= 0:
            return
        key = (q, semname)
        if self.waited.get(key, 0) >= val:
            return
        self.waited[key] = val
        h = self.sem[semname]
        self.tape.append((q, lambda e, h=h, v=val: e.wait_ge(h, v)))

    def _dep_waits(self, q, reads, writes):
        ev = []
        for b in reads:
            if b.writer is not None:
                ev.append(b.writer)
            if b.excl:
                ev.extend(b.readers)
        for b in writes:
            if b.writer is not None:
                ev.append(b.writer)
            ev.extend(b.readers)
        need = {}
        for semname, val in ev:
            if semname == "pe" and q == "pe":
                continue
            if val is None:
                raise RuntimeError(f"dependence on unfinished group (queue {q})")
            need[semname] = max(need.get(semname, 0), val)
        for semname, val in sorted(need.items()):
            self._emit_wait(q, semname, val)

    def _post(self, ev, reads, writes):
        for b in writes:
            b.writer = ev
            b.readers = []
        for b in reads:
            b.readers.append(ev)

    def op(self, q, fn, reads=(), writes=()):
        assert q in ("pe", "dve", "act", "pool")
        self._dep_waits(q, reads, writes)
        self.cnt[q] += 1
        h = self.sem[q]

        def run(e, fn=fn, h=h):
            fn(e).then_inc(h, 1)
        self.tape.append((q, run))
        ev = (q, self.cnt[q])
        self._post(ev, reads, writes)
        return ev

    def dma(self, q, sem, fn, reads=(), writes=()):
        assert q in ("sp", "pool")
        self._dep_waits(q, reads, writes)
        self._emit_wait(q, sem, self.cnt[sem])
        self.cnt[sem] += 16
        h = self.sem[sem]

        def run(e, fn=fn, h=h):
            fn(e).then_inc(h, 16)
        self.tape.append((q, run))
        ev = (sem, self.cnt[sem])
        self._post(ev, reads, writes)
        return ev

    def mm_group(self, mms, reads=(), writes=()):
        self._dep_waits("pe", reads, writes)
        self.cnt["pe"] += 1
        h = self.sem["pe"]
        n = len(mms)
        for i, fn in enumerate(mms):
            if i < n - 1:
                self.tape.append(("pe", lambda e, fn=fn: fn(e)))
            else:
                def run(e, fn=fn, h=h):
                    fn(e).then_inc(h, 1)
                self.tape.append(("pe", run))
        ev = ("pe", self.cnt["pe"])
        self._post(ev, reads, writes)
        return ev

    def barrier(self, qs=None):
        qs = qs or self.QUEUES
        snap = dict(self.cnt)
        for q in qs:
            for semname, val in snap.items():
                if val > 0 and semname != q:
                    self._emit_wait(q, semname, val)

    def replay(self, block):
        per = {q: [it for (qq, it) in self.tape if qq == q] for q in self.QUEUES}

        def section(items):
            def f(eng):
                for it in items:
                    it(eng)
            return f
        block.tensor(section(per["pe"]))
        block.vector(section(per["dve"]))
        block.scalar(section(per["act"]))
        block.sync(section(per["sp"]))
        block.gpsimd(section(per["pool"]))


_DTSIZE = {F32: 4, BF16: 2, U32: 4}


class Arena:

    BASE = 16384

    def __init__(self, nc, limit=207 * 1024 + 512):
        self.nc = nc
        self.limit = self.BASE + limit
        self.free_list = [(self.BASE, limit)]
        self.n = 0
        self.sizes = {}

    def alloc(self, shape, dtype, name=None):
        per = int(np.prod(shape[1:])) * _DTSIZE[dtype]
        per = (per + 31) & ~31
        name = name or f"sb{self.n}"
        self.n += 1
        for i, (off, sz) in enumerate(self.free_list):
            if sz >= per:
                h = self.nc.alloc_sbuf_tensor_at(name, list(shape), dtype, offset=off)
                if sz == per:
                    self.free_list.pop(i)
                else:
                    self.free_list[i] = (off + per, sz - per)
                self.sizes[id(h)] = (off, per)
                return h
        raise MemoryError(f"SBUF arena overflow: need {per} for {name}; free={self.free_list}")

    def free(self, *handles):
        for h in handles:
            off, per = self.sizes.pop(id(h))
            self.free_list.append((off, per))
        self.free_list.sort()
        merged = []
        for off, sz in self.free_list:
            if merged and merged[-1][0] + merged[-1][1] == off:
                merged[-1] = (merged[-1][0], merged[-1][1] + sz)
            else:
                merged.append((off, sz))
        self.free_list = [tuple(x) for x in merged]


_TABLES = None


def host_tables():
    global _TABLES
    if _TABLES is not None:
        return _TABLES
    i = np.arange(L, dtype=np.float64)
    f = np.arange(NFP, dtype=np.float64)
    ang = np.outer(i, f) * (2.0 * np.pi / L)
    cosA = np.cos(ang)
    sinA = np.sin(ang)
    cosA[:, NF:] = 0.0
    sinA[:, NF:] = 0.0
    wf = np.zeros(NFP)
    wf[0] = 1.0 / L
    wf[1:NF - 1] = 2.0 / L
    wf[NF - 1] = 1.0 / L
    angB = np.outer(f, i) * (2.0 * np.pi / L)
    cosB = np.cos(angB) * wf[:, None]
    sinB = np.sin(angB) * wf[:, None]
    cosB[NF:, :] = 0.0
    sinB[NF:, :] = 0.0
    inv = 1.0 / ROPE_THETA ** (np.arange(0, HD, 2, dtype=np.float64) / HD)
    a = np.outer(np.arange(S, dtype=np.float64), inv)
    rc = np.tile(np.cos(a), (1, HEADS)).astype(BF)
    rs = np.tile(np.sin(a), (1, HEADS)).astype(BF)
    q = np.arange(128)[:, None]
    k = np.arange(512)[None, :]
    masks = np.stack([np.where(k <= q + 128 * m, 0.0, MAXNEG) for m in range(4)], 1)
    ident = np.eye(128, dtype=np.float32).astype(BF)
    def tile_A(tab):
        return np.ascontiguousarray(
            tab.reshape(DC, 128, FC, 128).transpose(2, 1, 0, 3).reshape(FC, 128, DC * 128))

    def tile_B(tab):
        return np.ascontiguousarray(
            tab.reshape(FC, 128, 8, 512).transpose(2, 1, 0, 3).reshape(8, 128, FC * 512))

    cosA16, sinA16 = cosA.astype(BF), sinA.astype(BF)
    cosB16, sinB16 = cosB.astype(BF), sinB.astype(BF)
    _TABLES = dict(
        cosA=np.ascontiguousarray(cosA16), sinA=np.ascontiguousarray(sinA16),
        cosA_t=tile_A(cosA16), sinA_t=tile_A(sinA16),
        cosB_t=tile_B(cosB16), sinB_t=tile_B(sinB16),
        rope_cos=np.ascontiguousarray(rc), rope_sin=np.ascontiguousarray(rs),
        masks=np.ascontiguousarray(masks.reshape(128, 4 * 512).astype(np.float32)),
        ident=ident,
    )
    return _TABLES


def host_weights(inp):
    w = {}
    cat = np.concatenate([
        np.asarray(inp["L0_ffn_W_base"]).T,
        np.asarray(inp["L0_ffn_A_w"]).reshape(128, D).T,
        np.asarray(inp["L0_ffn_W_gate"]).T,
        np.asarray(inp["L0_ffn_W_thr"]).reshape(D, 1),
        np.zeros((D, B2_COLS - 4233), np.float32),
    ], axis=1)
    w["w_b2"] = np.ascontiguousarray(
        cat.astype(np.float32).reshape(DC, 128, B2_NBLK, 512).transpose(2, 1, 0, 3)
        .reshape(B2_NBLK, 128, DC * 512))
    b2 = np.zeros((1, B2_COLS), np.float32)
    b2[0, :4096] = np.asarray(inp["L0_ffn_b_base"])
    b2[0, OFF_THR] = np.asarray(inp["L0_ffn_b_thr"])[0]
    w["b_b2"] = b2
    w["w_B0"] = np.ascontiguousarray(np.asarray(inp["L0_ffn_B_w"]).transpose(0, 2, 1).reshape(128, D), np.float32)
    qcat = np.concatenate([
        np.asarray(inp["L1_attn_w_qkv"]).T,
        np.asarray(inp["L1_ffn_lora_A"]).reshape(128, D).T,
        np.zeros((D, QKV_COLS - 6272), np.float32),
    ], axis=1)
    w["w_qkv"] = np.ascontiguousarray(
        qcat.astype(np.float32).reshape(DC, 128, QKV_NBLK, 512).transpose(2, 1, 0, 3)
        .reshape(QKV_NBLK, 128, DC * 512))
    qb = np.zeros((1, QKV_COLS), np.float32)
    qb[0, :QKVD] = np.asarray(inp["L1_attn_b_qkv"])
    w["b_qkv"] = qb
    w["w_wo"] = np.ascontiguousarray(np.asarray(inp["L1_attn_w_o"]).T.astype(np.float32).reshape(DC, 128, 8, 512).transpose(2, 1, 0, 3).reshape(8, 128, DC * 512))
    w["b_wo"] = np.ascontiguousarray(np.asarray(inp["L1_attn_b_o"]).reshape(1, D), np.float32)
    w["w_w1"] = np.ascontiguousarray(np.asarray(inp["L1_ffn_W"]).T.astype(np.float32).reshape(DC, 128, 8, 512).transpose(2, 1, 0, 3).reshape(8, 128, DC * 512))
    w["b_w1"] = np.ascontiguousarray(np.asarray(inp["L1_ffn_bias"]).reshape(1, D), np.float32)
    w["w_B1"] = np.ascontiguousarray(np.asarray(inp["L1_ffn_lora_B"]).transpose(0, 2, 1).reshape(128, D), np.float32)
    w["lora_w"] = np.ascontiguousarray(np.asarray(inp["L1_ffn_lora_weights"]).reshape(1, 8), np.float32)
    return w


def shard_rows(core):
    return core // 2, [core % 2 + 2 * t for t in range(NTILE)]


def shard_x(x, core):
    s, tiles = shard_rows(core)
    xr = np.asarray(x[s]).reshape(16, 128, D)
    return np.ascontiguousarray(xr[tiles].reshape(NTOK, D))


def unshard(outs):
    full = np.zeros((Bsz, S, D), np.float32)
    for c in range(NCORE):
        s, tiles = shard_rows(c)
        o = np.asarray(outs[c]).reshape(NTILE, 128, D)
        fr = full[s].reshape(16, 128, D)
        for i, t in enumerate(tiles):
            fr[t] = o[i]
    return full


class Prog:
    def __init__(self, nc, es, core):
        self.nc, self.es, self.core = nc, es, core
        self.em = Em(nc, es)
        self.ar = Arena(nc)
        dr = nc.dram_tensor
        self.d_ident = dr("ident", [128, 128], BF16, kind="ExternalInput").ap()
        self.d_identf = dr("identf", [128, 128], F32, kind="ExternalInput").ap()
        self.ident = self.ar.alloc([128, 128], BF16, "ident")
        self.b_ident = Buf("ident")
        self.identf = self.ar.alloc([128, 128], F32, "identf")
        self.b_identf = Buf("identf")
        self.ones1 = self.ar.alloc([1, 128], BF16, "ones1")
        self.b_ones = Buf("ones1")
        self.zero1 = self.ar.alloc([128, 1], F32, "zero1")
        self.b_zero1 = Buf("zero1")
        self.eps1 = self.ar.alloc([128, 1], F32, "eps1")
        self.b_eps1 = Buf("eps1")
        self.ps_acc = nc.alloc_psum_tensor("ps_acc", [128, 2048], F32)
        self.ps_b4 = nc.alloc_psum_tensor("ps_b4", [128, 512], F32)
        self.ps_b5 = nc.alloc_psum_tensor("ps_b5", [128, 512], F32)
        self.ps_t0 = nc.alloc_psum_tensor("ps_t0", [128, 1024], BF16)
        self.ps_t1 = nc.alloc_psum_tensor("ps_t1", [128, 1024], BF16)
        self.b_acc = [Buf(f"acc{i}", excl=True) for i in range(4)]
        self.b_b4, self.b_b5 = Buf("b4", excl=True), Buf("b5", excl=True)
        self.b_t = [Buf("t0", excl=True), Buf("t1", excl=True)]
        self.dbg = {}

    def prelude(self):
        em = self.em
        em.dma("sp", "dsp0", lambda e: e.dma_start(out=self.ident[:], in_=self.d_ident),
               writes=[self.b_ident])
        em.dma("sp", "dsp0", lambda e: e.dma_start(out=self.identf[:], in_=self.d_identf),
               writes=[self.b_identf])
        em.op("dve", lambda e: e.memset(self.ones1[:], 1.0), writes=[self.b_ones])
        em.op("dve", lambda e: e.memset(self.zero1[:], 0.0), writes=[self.b_zero1])
        em.op("dve", lambda e: e.memset(self.eps1[:], 1e-5), writes=[self.b_eps1])

    def tap(self, name, ap, shape, dtype, reads, force=False):
        if not (force or DBG.get(name)):
            return
        key = "dbg_" + name
        d = self.nc.dram_tensor(key, list(shape), dtype, kind="ExternalOutput").ap()
        b = Buf(key)
        self.dbg[key] = b
        self.em.dma("sp", "dst0", lambda e: e.dma_start(out=d, in_=ap), reads=list(reads), writes=[b])

    def trans128(self, src_ap, dst_ap, b_src, b_dst, idx, f32=False, copy_eng="dve"):
        em = self.em
        if f32:
            ps, tb = (self.ps_b4, self.b_b4) if idx % 2 == 0 else (self.ps_b5, self.b_b5)
            psl = ps[:, 0:128]
            idap, bid = self.identf[:], self.b_identf
        else:
            ps, tb = (self.ps_t0, self.b_t[0]) if idx % 2 == 0 else (self.ps_t1, self.b_t[1])
            psl = ps[:, 0:128]
            idap, bid = self.ident[:], self.b_ident
        em.op("pe", lambda e: e.transpose(psl, src_ap, idap),
              reads=[b_src, bid], writes=[tb])
        em.op(copy_eng, lambda e: (e.tensor_copy(out=dst_ap, in_=psl) if copy_eng != "act"
                                   else e.activation(out=dst_ap, in_=psl, func=AF.Identity, bias=self.zero1[:])),
              reads=[tb] + ([self.b_zero1] if copy_eng == "act" else []), writes=[b_dst])


def layernorm_tile(p, src_ap, b_src, out_ap, b_out, scratch):
    em = p.em
    stats, b_stats, mv, b_mv, rstd, b_rstd, nbias, b_nbias = scratch
    for c8 in range(8):
        em.op("dve", lambda e, c8=c8: e.bn_stats(
            out=stats[:, 6 * c8:6 * c8 + 6], in_=src_ap[:, 512 * c8:512 * (c8 + 1)]),
            reads=[b_src], writes=[b_stats])
    em.op("dve", lambda e: e.bn_aggr(out=mv[:], in_=stats[:]), reads=[b_stats], writes=[b_mv])
    em.op("act", lambda e: e.activation(out=rstd[:], in_=mv[:, 1:2], func=AF.Ln, bias=p.eps1[:]),
          reads=[b_mv, p.b_eps1], writes=[b_rstd])
    em.op("act", lambda e: e.activation(out=rstd[:], in_=rstd[:], func=AF.Exp, scale=-0.5, bias=p.zero1[:]),
          reads=[b_rstd, p.b_zero1], writes=[b_rstd])
    em.op("dve", lambda e: e.scalar_tensor_tensor(
        out=nbias[:], in0=mv[:, 0:1], scalar=-1.0, in1=rstd[:], op0=ALU.mult, op1=ALU.mult),
        reads=[b_mv, b_rstd], writes=[b_nbias])
    em.op("act", lambda e: e.activation(
        out=out_ap, in_=src_ap, func=AF.Identity, bias=nbias[:], scale=rstd[:]),
        reads=[b_src, b_rstd, b_nbias], writes=[b_out])


def build_l0(p):
    nc, em, ar = p.nc, p.em, p.ar

    dr = nc.dram_tensor
    p.d_x = dr("x_in", [NTOK, D], F32, kind="ExternalInput").ap()
    p.d_cosA = dr("cosA", [L, NFP], BF16, kind="ExternalInput").ap()
    p.d_sinA = dr("sinA", [L, NFP], BF16, kind="ExternalInput").ap()
    p.d_cosA_t = dr("cosA_t", [FC, 128, DC * 128], BF16, kind="ExternalInput").ap()
    p.d_sinA_t = dr("sinA_t", [FC, 128, DC * 128], BF16, kind="ExternalInput").ap()
    p.d_cosB_t = dr("cosB_t", [8, 128, FC * 512], BF16, kind="ExternalInput").ap()
    p.d_sinB_t = dr("sinB_t", [8, 128, FC * 512], BF16, kind="ExternalInput").ap()
    ext = "ExternalOutput" if DBG.get("l0_out") else "Internal"
    p.d_x1 = nc.dram_tensor("x1_d", [NTOK, D], F32, kind=ext).ap()
    p.b_x1d = Buf("x1_d")
    p.d_base0 = nc.dram_tensor("base0_d", [NTOK, D], BF16, kind="Internal").ap()
    p.b_base0d = Buf("base0_d")
    p.d_wb2 = nc.dram_tensor("w_b2", [B2_NBLK, 128, DC * 512], F32, kind="ExternalInput").ap()
    p.d_bb2 = nc.dram_tensor("b_b2", [1, B2_COLS], F32, kind="ExternalInput").ap()
    p.d_B0 = nc.dram_tensor("w_B0", [128, D], F32, kind="ExternalInput").ap()

    h0 = ar.alloc([128, NTILE, 128], F32, "h0")
    b_h0 = Buf("h0")
    gate = ar.alloc([128, NTILE, 8], F32, "gate")
    b_gate = Buf("gate")
    thr = ar.alloc([128, NTILE, 1], F32, "thr")
    b_thr = Buf("thr")
    m8 = ar.alloc([128, NTILE, 8], F32, "m8")
    b_m8 = Buf("m8")
    i8 = ar.alloc([128, NTILE, 8], U32, "i8")
    b_i8 = Buf("i8")
    wn = ar.alloc([128, NTILE, 8], F32, "wn")
    b_wn = Buf("wn")
    lnscratch = (ar.alloc([128, 48], F32, "stats"), Buf("stats"),
                 ar.alloc([128, 2], F32, "mv"), Buf("mv"),
                 ar.alloc([128, 1], F32, "rstd"), Buf("rstd"),
                 ar.alloc([128, 1], F32, "nbias"), Buf("nbias"))
    scr = (lnscratch[0], lnscratch[1], lnscratch[2], lnscratch[3],
           lnscratch[4], lnscratch[5], lnscratch[6], lnscratch[7])

    xnT = ar.alloc([128, DC, NTOK], BF16, "xnT")
    b_xnT = Buf("xnT")
    xa = ar.alloc([128, 2, D], F32, "xtile")
    b_xa = [Buf("xa0"), Buf("xa1")]
    xn = ar.alloc([128, D], BF16, "xn")
    b_xn = Buf("xn")
    for ti in range(NTILE):
        xab, b_x = xa[:, ti % 2, :], b_xa[ti % 2]
        em.dma("sp", "dsp" + str(ti % 2), lambda e, ti=ti, xab=xab: e.dma_start(
            out=xab, in_=p.d_x[ti * 128:(ti + 1) * 128, :]), writes=[b_x])
        layernorm_tile(p, xab, b_x, xn[:], b_xn, scr)
        for dc in range(DC):
            p.trans128(xn[:, dc * 128:(dc + 1) * 128],
                       xnT[:, dc, ti * 128:(ti + 1) * 128], b_xn, b_xnT, dc)
    ar.free(xa, xn)
    em.barrier()
    if DBG.get("stop_after_A"):
        p.tap("xnT_A", xnT[:], [128, DC * NTOK], BF16, [b_xnT], force=True)
        em.barrier()
        return
    if DBG.get("xnT"):
        p.tap("xnT", xnT[:], [128, DC * NTOK], BF16, [b_xnT])

    wblk = ar.alloc([128, 2, DC, 512], BF16, "wblk")
    b_w = [Buf("w0"), Buf("w1")]
    brow = ar.alloc([1, B2_COLS], BF16, "brow")
    b_brow = Buf("brow")
    evac = ar.alloc([128, 2, 512], BF16, "evac")
    b_evac = [Buf("ev0"), Buf("ev1")]
    em.dma("pool", "dw2", lambda e: e.dma_start(out=brow[:], in_=p.d_bb2), writes=[b_brow])
    for blk in range(DBG.get("b2_blocks", B2_NBLK)):
        wb, bw = wblk[:, blk % 2], b_w[blk % 2]
        em.dma("pool", "dw" + str(blk % 2), lambda e, blk=blk, wb=wb: e.dma_start(
            out=wb, in_=p.d_wb2[blk].rearrange("p (c n) -> p c n", c=DC)),
            writes=[bw])
        for ti in range(NTILE):
            rb4 = (blk * NTILE + ti) % 4
            bank = p.b_acc[rb4]
            psl = p.ps_acc[:, rb4 * 512:(rb4 + 1) * 512]
            mms = []
            for kc in range(DC):
                mms.append(lambda e, kc=kc, ti=ti, wb=wb, psl=psl: e.matmul(
                    psl, xnT[:, kc, ti * 128:(ti + 1) * 128], wb[:, kc, :],
                    start=(kc == 0), stop=False))
            if DBG.get("no_bias"):
                mms[-1] = (lambda f: (lambda e: f(e, True)))(
                    (lambda e, kc=DC - 1, ti=ti, wb=wb, psl=psl: None)) if False else mms[-1]
                kc = DC - 1
                mms[-1] = lambda e, kc=kc, ti=ti, wb=wb, psl=psl: e.matmul(
                    psl, xnT[:, kc, ti * 128:(ti + 1) * 128], wb[:, kc, :],
                    start=False, stop=True)
            else:
                mms.append(lambda e, blk=blk, psl=psl: e.matmul(
                    psl, p.ones1[:], brow[:, blk * 512:(blk + 1) * 512], start=False, stop=True))
            em.mm_group(mms, reads=[b_xnT, bw, p.b_ones, b_brow], writes=[bank])
            if blk < 8:
                ev, bev = evac[:, ti % 2, :], b_evac[ti % 2]
                em.op("act", lambda e, psl=psl, ev=ev: e.activation(out=ev, in_=psl, func=AF.Identity, bias=p.zero1[:]),
                      reads=[bank], writes=[bev])
                em.dma("sp", "dst" + str(ti % 2), lambda e, blk=blk, ti=ti, ev=ev: e.dma_start(
                    out=p.d_base0[ti * 128:(ti + 1) * 128, blk * 512:(blk + 1) * 512], in_=ev),
                    reads=[bev], writes=[p.b_base0d])
            else:
                if not DBG.get("skip_h0"):
                    em.op("act", lambda e, ti=ti, psl=psl: e.activation(
                        out=h0[:, ti, :], in_=psl[:, 0:128], func=AF.Identity, bias=p.zero1[:]),
                        reads=[bank], writes=[b_h0])
                if not DBG.get("skip_gate"):
                    em.op("dve", lambda e, ti=ti, psl=psl: e.tensor_copy(
                        out=gate[:, ti, :], in_=psl[:, 128:136]), reads=[bank], writes=[b_gate])
                if not DBG.get("skip_thr"):
                    em.op("dve", lambda e, ti=ti, psl=psl: e.tensor_copy(
                        out=thr[:, ti, :], in_=psl[:, 136:137]), reads=[bank], writes=[b_thr])
    ar.free(wblk, brow, evac)
    em.barrier()
    if DBG.get("stop_after_B2"):
        p.tap("h0_B2", h0[:], [128, NTILE * 128], F32, [b_h0], force=True)
        em.barrier()
        return

    RI = ar.alloc([128, 2, FC, NTOK], BF16, "RI")
    b_RI = Buf("RI")
    tabA = ar.alloc([128, 2, 2, DC, 128], BF16, "tabA")
    b_tA = [Buf("tA0"), Buf("tA1")]
    for fb in range(FC):
        tb, btb = tabA[:, fb % 2], b_tA[fb % 2]
        em.dma("pool", "dw" + str(2 * (fb % 2)), lambda e, fb=fb, tb=tb: e.dma_start(
            out=tb[:, 0], in_=p.d_cosA_t[fb].rearrange("p (c n) -> p c n", c=DC)),
            writes=[btb])
        em.dma("pool", "dw" + str(2 * (fb % 2) + 1), lambda e, fb=fb, tb=tb: e.dma_start(
            out=tb[:, 1], in_=p.d_sinA_t[fb].rearrange("p (c n) -> p c n", c=DC)),
            writes=[btb])
        for cs in range(2):
            for th in range(2):
                rb4 = (fb * 4 + cs * 2 + th) % 4
                bank = p.b_acc[rb4]
                psl = p.ps_acc[:, rb4 * 512:(rb4 + 1) * 512]
                mms = []
                for kc in range(DC):
                    mms.append(lambda e, kc=kc, cs=cs, th=th, tb=tb, psl=psl: e.matmul(
                        psl, tb[:, cs, kc, :], xnT[:, kc, th * 512:(th + 1) * 512],
                        start=(kc == 0), stop=(kc == DC - 1)))
                em.mm_group(mms, reads=[b_xnT, btb], writes=[bank])
                em.op("act", lambda e, fb=fb, cs=cs, th=th, psl=psl: e.activation(
                    out=RI[:, cs, fb, th * 512:(th + 1) * 512], in_=psl, func=AF.Identity, bias=p.zero1[:]),
                    reads=[bank], writes=[b_RI])
    if DBG.get("RI"):
        p.tap("RI", RI[:], [128, 2 * FC * NTOK], BF16, [b_RI])
    ar.free(tabA, xnT)
    em.barrier()
    if DBG.get("stop_after_B1"):
        ar.free(RI)
        em.barrier()
        return

    P = ar.alloc([128, FC, NTOK], BF16, "P")
    b_P = Buf("P")
    ptmp = ar.alloc([128, NTOK], F32, "ptmp")
    b_pt = Buf("ptmp")
    for fcc in range(FC):
        em.op("dve", lambda e, fcc=fcc: e.tensor_tensor(
            out=ptmp[:], in0=RI[:, 0, fcc, :], in1=RI[:, 0, fcc, :], op=ALU.mult),
            reads=[b_RI], writes=[b_pt])
        em.op("dve", lambda e, fcc=fcc: e.scalar_tensor_tensor(
            out=P[:, fcc, :], in0=RI[:, 1, fcc, :], scalar=1.0, in1=RI[:, 1, fcc, :],
            op0=ALU.mult, op1=ALU.mult),
            reads=[b_RI], writes=[b_P])
        em.op("dve", lambda e, fcc=fcc: e.tensor_tensor(
            out=P[:, fcc, :], in0=P[:, fcc, :], in1=ptmp[:], op=ALU.add),
            reads=[b_pt, b_P], writes=[b_P])
    ar.free(ptmp)
    em.barrier()

    corr = ar.alloc([128, NTILE, L], BF16, "corr")
    b_corr = Buf("corr")
    tabB = ar.alloc([128, FC, 512], BF16, "tabB")
    b_tB = Buf("tB")
    for db in range(8):
        tb, btb = tabB, b_tB
        em.dma("pool", "dw0", lambda e, db=db, tb=tb: e.dma_start(
            out=tb[:], in_=p.d_cosB_t[db].rearrange("p (c n) -> p c n", c=FC)),
            writes=[btb])
        for ti in range(NTILE):
            rb4 = (db * NTILE + ti) % 4
            bank = p.b_acc[rb4]
            psl = p.ps_acc[:, rb4 * 512:(rb4 + 1) * 512]
            mms = []
            for fcc in range(FC):
                mms.append(lambda e, fcc=fcc, ti=ti, tb=tb, psl=psl: e.matmul(
                    psl, P[:, fcc, ti * 128:(ti + 1) * 128], tb[:, fcc, :],
                    start=(fcc == 0), stop=(fcc == FC - 1)))
            em.mm_group(mms, reads=[b_P, btb], writes=[bank])
            em.op("act", lambda e, db=db, ti=ti, psl=psl: e.activation(
                out=corr[:, ti, db * 512:(db + 1) * 512], in_=psl, func=AF.Identity, bias=p.zero1[:]),
                reads=[bank], writes=[b_corr])
    ar.free(P, tabB)
    em.barrier()
    if DBG.get("corr"):
        p.tap("corr", corr[:], [128, NTILE * L], BF16, [b_corr])

    sm = ar.alloc([128, 8], F32, "smtmp")
    b_sm = Buf("smtmp")
    nv0 = ar.alloc([128, 1], F32, "nv0")
    b_nv0 = Buf("nv0")
    wsum = ar.alloc([128, 1], F32, "wsum")
    b_ws = Buf("wsum")
    winv = ar.alloc([128, 1], F32, "winv")
    b_wi = Buf("winv")
    def topk_tile(ti):
        em.op("dve", lambda e, ti=ti: e.max(out=m8[:, ti, :], in_=corr[:, ti, :]),
              reads=[b_corr], writes=[b_m8])
        yield
        em.op("dve", lambda e, ti=ti: e.max_index(out=i8[:, ti, :], in_max=m8[:, ti, :], in_values=corr[:, ti, :]),
              reads=[b_corr, b_m8], writes=[b_i8])
        yield
        em.op("dve", lambda e, ti=ti: e.tensor_scalar(
            out=nv0[:], in0=m8[:, ti, 0:1], scalar1=-1.0, scalar2=None, op0=ALU.mult),
            reads=[b_m8], writes=[b_nv0])
        em.op("act", lambda e, ti=ti: e.activation(
            out=sm[:], in_=m8[:, ti, :], func=AF.Exp, bias=nv0[:], accum_out=wsum[:]),
            reads=[b_m8, b_nv0], writes=[b_sm, b_ws])
        em.op("dve", lambda e: e.reciprocal(out=winv[:], in_=wsum[:]), reads=[b_ws], writes=[b_wi])
        em.op("dve", lambda e, ti=ti: e.tensor_scalar(
            out=wn[:, ti, :], in0=sm[:], scalar1=winv[:], scalar2=None, op0=ALU.mult),
            reads=[b_sm, b_wi], writes=[b_wn])
        yield
    tk_live = [topk_tile(0), topk_tile(1)]
    tk_next = 2
    while tk_live:
        keep = []
        for g in tk_live:
            try:
                next(g)
                keep.append(g)
            except StopIteration:
                pass
        tk_live = keep
        if len(tk_live) < 2 and tk_next < NTILE:
            tk_live.append(topk_tile(tk_next))
            tk_next += 1
    ar.free(sm, nv0, wsum, winv, corr)
    em.barrier()
    if DBG.get("topk"):
        p.tap("topk_m8", m8[:], [128, NTILE * 8], F32, [b_m8], force=True)
        p.tap("topk_i8", i8[:], [128, NTILE * 8], U32, [b_i8], force=True)
        p.tap("topk_wn", wn[:], [128, NTILE * 8], F32, [b_wn], force=True)

    Y = ar.alloc([128, 2, FC, NTOK], BF16, "Y")
    b_Y = Buf("Y")
    Gc = ar.alloc([128, 2, 2, NFP], BF16, "Gc")
    Gs = ar.alloc([128, 2, 2, NFP], BF16, "Gs")
    b_G = [[Buf(f"g{t}{i}") for i in range(2)] for t in range(2)]
    Kc = ar.alloc([128, 2, NFP], BF16, "Kc")
    Ks = ar.alloc([128, 2, NFP], BF16, "Ks")
    b_Kc = [Buf("Kc0"), Buf("Kc1")]
    b_Ks = [Buf("Ks0"), Buf("Ks1")]
    KT = ar.alloc([128, 1, 2, FC, 128], BF16, "KT")
    b_KT = [Buf("KT0"), Buf("KT0b")]
    ytmp = ar.alloc([128, 2, 2, 128], F32, "ytmp")
    b_yt = [[Buf(f"yt{t}{i}") for i in range(2)] for t in range(2)]

    def mixer_tile(ti):
        tp = ti % 2
        bKc, bKs, bKT = b_Kc[tp], b_Ks[tp], b_KT[0]
        kc_t, ks_t = Kc[:, tp, :], Ks[:, tp, :]
        ktt = KT[:, 0]
        for j in range(8):
            gp = j % 2
            gc, gs, bg = Gc[:, tp, gp, :], Gs[:, tp, gp, :], b_G[tp][gp]
            em.dma("pool", "dg" + str(2 * tp), lambda e, ti=ti, j=j, gc=gc: e.indirect_dma_start(
                out=gc, out_offset=None, in_=p.d_cosA,
                in_offset=IndirectOffsetOnAxis(ap=i8[:, ti, j:j + 1], axis=0)),
                reads=[b_i8], writes=[bg])
            em.dma("pool", "dg" + str(2 * tp + 1), lambda e, ti=ti, j=j, gs=gs: e.indirect_dma_start(
                out=gs, out_offset=None, in_=p.d_sinA,
                in_offset=IndirectOffsetOnAxis(ap=i8[:, ti, j:j + 1], axis=0)),
                reads=[b_i8], writes=[bg])
            em.op("act", lambda e, ti=ti, j=j, gc=gc: e.activation(
                out=gc, in_=gc, func=AF.Identity, scale=wn[:, ti, j:j + 1], bias=p.zero1[:]),
                reads=[b_wn, p.b_zero1, bg], writes=[bg])
            em.op("act", lambda e, ti=ti, j=j, gs=gs: e.activation(
                out=gs, in_=gs, func=AF.Identity, scale=wn[:, ti, j:j + 1], bias=p.zero1[:]),
                reads=[b_wn, p.b_zero1, bg], writes=[bg])
            if j == 0:
                em.op("dve", lambda e, gc=gc, kc_t=kc_t: e.tensor_copy(out=kc_t, in_=gc),
                      reads=[bg], writes=[bKc])
                em.op("dve", lambda e, gs=gs, ks_t=ks_t: e.tensor_copy(out=ks_t, in_=gs),
                      reads=[bg], writes=[bKs])
            else:
                em.op("dve", lambda e, gc=gc, kc_t=kc_t: e.tensor_tensor(
                    out=kc_t, in0=kc_t, in1=gc, op=ALU.add), reads=[bg, bKc], writes=[bKc])
                em.op("dve", lambda e, gs=gs, ks_t=ks_t: e.tensor_tensor(
                    out=ks_t, in0=ks_t, in1=gs, op=ALU.add), reads=[bg, bKs], writes=[bKs])
            if j % 2 == 1:
                yield
        for fcc in range(FC):
            p.trans128(kc_t[:, fcc * 128:(fcc + 1) * 128], ktt[:, 0, fcc, :], bKc, bKT, 2 * fcc + tp)
            p.trans128(ks_t[:, fcc * 128:(fcc + 1) * 128], ktt[:, 1, fcc, :], bKs, bKT, 2 * fcc + 1 + tp)
            if fcc % 4 == 3:
                yield
        for fcc in range(FC):
            yt0, byt0 = ytmp[:, tp, 0, :], b_yt[tp][0]
            yt1, byt1 = ytmp[:, tp, 1, :], b_yt[tp][1]
            em.op("dve", lambda e, fcc=fcc, ti=ti, ktt=ktt: e.tensor_tensor(
                out=Y[:, 0, fcc, ti * 128:(ti + 1) * 128],
                in0=RI[:, 0, fcc, ti * 128:(ti + 1) * 128], in1=ktt[:, 0, fcc, :], op=ALU.mult),
                reads=[b_RI, bKT], writes=[b_Y])
            em.op("dve", lambda e, fcc=fcc, ti=ti, ktt=ktt, yt0=yt0: e.tensor_tensor(
                out=yt0, in0=RI[:, 1, fcc, ti * 128:(ti + 1) * 128], in1=ktt[:, 1, fcc, :], op=ALU.mult),
                reads=[b_RI, bKT], writes=[byt0])
            em.op("dve", lambda e, fcc=fcc, ti=ti, yt0=yt0: e.tensor_tensor(
                out=Y[:, 0, fcc, ti * 128:(ti + 1) * 128],
                in0=Y[:, 0, fcc, ti * 128:(ti + 1) * 128], in1=yt0, op=ALU.add),
                reads=[byt0, b_Y], writes=[b_Y])
            em.op("dve", lambda e, fcc=fcc, ti=ti, ktt=ktt: e.tensor_tensor(
                out=Y[:, 1, fcc, ti * 128:(ti + 1) * 128],
                in0=RI[:, 1, fcc, ti * 128:(ti + 1) * 128], in1=ktt[:, 0, fcc, :], op=ALU.mult),
                reads=[b_RI, bKT], writes=[b_Y])
            em.op("dve", lambda e, fcc=fcc, ti=ti, ktt=ktt, yt1=yt1: e.tensor_tensor(
                out=yt1, in0=RI[:, 0, fcc, ti * 128:(ti + 1) * 128], in1=ktt[:, 1, fcc, :], op=ALU.mult),
                reads=[b_RI, bKT], writes=[byt1])
            em.op("dve", lambda e, fcc=fcc, ti=ti, yt1=yt1: e.tensor_tensor(
                out=Y[:, 1, fcc, ti * 128:(ti + 1) * 128],
                in0=Y[:, 1, fcc, ti * 128:(ti + 1) * 128], in1=yt1, op=ALU.subtract),
                reads=[byt1, b_Y], writes=[b_Y])
            if fcc % 4 == 3:
                yield

    t_live = [mixer_tile(0)]
    nxt_tile = 1
    while t_live:
        keep = []
        for g in t_live:
            try:
                next(g)
                keep.append(g)
            except StopIteration:
                pass
        t_live = keep
        if len(t_live) < 2 and nxt_tile < NTILE:
            t_live.append(mixer_tile(nxt_tile))
            nxt_tile += 1
    ar.free(Gc, Gs, Kc, Ks, KT, ytmp, RI)
    em.barrier()
    if DBG.get("Y"):
        p.tap("Y", Y[:], [128, 2 * FC * NTOK], BF16, [b_Y])

    B0 = ar.alloc([128, D], BF16, "B0")
    b_B0 = Buf("B0")
    em.dma("pool", "dw3", lambda e: e.dma_start(out=B0[:], in_=p.d_B0), writes=[b_B0])
    h0T = ar.alloc([128, NTILE, 128], BF16, "h0T")
    b_h0T = Buf("h0T")
    gm = ar.alloc([128, 1], F32, "gm")
    b_gm = Buf("gm")
    ge = ar.alloc([128, 8], F32, "ge")
    b_ge = Buf("ge")
    gs_ = ar.alloc([128, 1], F32, "gs")
    b_gs = Buf("gs")
    gi = ar.alloc([128, 1], F32, "gi")
    b_gi = Buf("gi")
    thrs = ar.alloc([128, 1], F32, "thrs")
    b_thrs = Buf("thrs")
    wm = ar.alloc([128, 8], F32, "wm")
    b_wm = Buf("wm")
    h0s = ar.alloc([128, 128], BF16, "h0s")
    b_h0s = Buf("h0s")
    for ti in range(NTILE):
        em.op("dve", lambda e, ti=ti: e.reduce_max(out=gm[:], in_=gate[:, ti, :], axis=AX, negate=True),
              reads=[b_gate], writes=[b_gm])
        em.op("act", lambda e, ti=ti: e.activation(
            out=ge[:], in_=gate[:, ti, :], func=AF.Exp, bias=gm[:], accum_out=gs_[:]),
            reads=[b_gate, b_gm], writes=[b_ge, b_gs])
        em.op("dve", lambda e: e.reciprocal(out=gi[:], in_=gs_[:]), reads=[b_gs], writes=[b_gi])
        em.op("act", lambda e, ti=ti: e.activation(out=thrs[:], in_=thr[:, ti, :], func=AF.Sigmoid, bias=p.zero1[:]),
              reads=[b_thr], writes=[b_thrs])
        em.op("dve", lambda e: e.tensor_scalar(
            out=thrs[:], in0=thrs[:], scalar1=-0.125, scalar2=None, op0=ALU.mult),
            reads=[b_thrs], writes=[b_thrs])
        em.op("dve", lambda e: e.tensor_scalar(
            out=wm[:], in0=ge[:], scalar1=gi[:], scalar2=thrs[:], op0=ALU.mult, op1=ALU.add),
            reads=[b_ge, b_gi, b_thrs], writes=[b_wm])
        em.op("dve", lambda e: e.tensor_scalar_max(wm[:], wm[:], 0.0), reads=[b_wm], writes=[b_wm])
        em.op("dve", lambda e: e.reduce_sum(out=gs_[:], in_=wm[:], axis=AX), reads=[b_wm], writes=[b_gs])
        em.op("dve", lambda e: e.tensor_scalar_max(gs_[:], gs_[:], 1e-30), reads=[b_gs], writes=[b_gs])
        em.op("dve", lambda e: e.reciprocal(out=gi[:], in_=gs_[:]), reads=[b_gs], writes=[b_gi])
        em.op("dve", lambda e: e.tensor_scalar(
            out=wm[:], in0=wm[:], scalar1=gi[:], scalar2=2.0, op0=ALU.mult, op1=ALU.mult),
            reads=[b_wm, b_gi], writes=[b_wm])
        for ee in range(8):
            em.op("dve", lambda e, ti=ti, ee=ee: e.tensor_scalar(
                out=h0s[:, ee * 16:(ee + 1) * 16], in0=h0[:, ti, ee * 16:(ee + 1) * 16],
                scalar1=wm[:, ee:ee + 1], scalar2=None, op0=ALU.mult),
                reads=[b_h0, b_wm], writes=[b_h0s])
        p.trans128(h0s[:], h0T[:, ti, :], b_h0s, b_h0T, ti)
    ar.free(gm, ge, gs_, gi, thrs, wm, h0s)
    em.barrier()
    if DBG.get("h0T"):
        p.tap("h0T", h0T[:], [128, NTILE * 128], BF16, [b_h0T])

    tabB2 = ar.alloc([128, 2, 2, FC, 512], BF16, "tabB2")
    b_tB2 = [Buf("tb20"), Buf("tb21")]
    bslice = ar.alloc([128, 2, 512], BF16, "bslice")
    b_bs = [Buf("bs0"), Buf("bs1")]
    xslice = ar.alloc([128, 2, 512], F32, "xslice")
    b_xs = [Buf("xs0"), Buf("xs1")]
    gsil = ar.alloc([128, 512], F32, "gsil")
    b_gsil = Buf("gsil")
    ot = ar.alloc([128, 2, 512], F32, "ot")
    b_ot = [Buf("ot0"), Buf("ot1")]
    for ob in range(8):
        tb, btb = tabB2[:, ob % 2], b_tB2[ob % 2]
        em.dma("pool", "dw" + str(2 * (ob % 2)), lambda e, ob=ob, tb=tb: e.dma_start(
            out=tb[:, 0], in_=p.d_cosB_t[ob].rearrange("p (c n) -> p c n", c=FC)),
            writes=[btb])
        em.dma("pool", "dw" + str(2 * (ob % 2) + 1), lambda e, ob=ob, tb=tb: e.dma_start(
            out=tb[:, 1], in_=p.d_sinB_t[ob].rearrange("p (c n) -> p c n", c=FC)),
            writes=[btb])
        for ti in range(NTILE):
            par = ti % 2
            bank_s = p.b_acc[par]
            psl_s = p.ps_acc[:, par * 512:(par + 1) * 512]
            bank_m = p.b_acc[2 + par]
            psl_m = p.ps_acc[:, (2 + par) * 512:(3 + par) * 512]
            bsl, bbs = bslice[:, par], b_bs[par]
            xsl, bxs = xslice[:, par], b_xs[par]
            osl, bot = ot[:, par], b_ot[par]
            em.dma("sp", "dsp" + str(par), lambda e, ob=ob, ti=ti, bsl=bsl: e.dma_start(
                out=bsl, in_=p.d_base0[ti * 128:(ti + 1) * 128, ob * 512:(ob + 1) * 512]),
                reads=[p.b_base0d], writes=[bbs])
            em.dma("sp", "dsp" + str(2 + par), lambda e, ob=ob, ti=ti, xsl=xsl: e.dma_start(
                out=xsl, in_=p.d_x[ti * 128:(ti + 1) * 128, ob * 512:(ob + 1) * 512]),
                writes=[bxs])
            mms = []
            for fcc in range(FC):
                mms.append(lambda e, fcc=fcc, ti=ti, tb=tb, psl_s=psl_s: e.matmul(
                    psl_s, Y[:, 0, fcc, ti * 128:(ti + 1) * 128], tb[:, 0, fcc, :],
                    start=(fcc == 0), stop=False))
                mms.append(lambda e, fcc=fcc, ti=ti, tb=tb, psl_s=psl_s: e.matmul(
                    psl_s, Y[:, 1, fcc, ti * 128:(ti + 1) * 128], tb[:, 1, fcc, :],
                    start=False, stop=(fcc == FC - 1)))
            em.mm_group(mms, reads=[b_Y, btb], writes=[bank_s])
            em.mm_group([lambda e, ob=ob, ti=ti, psl_m=psl_m: e.matmul(
                psl_m, h0T[:, ti, :], B0[:, ob * 512:(ob + 1) * 512], start=True, stop=True)],
                reads=[b_h0T, b_B0], writes=[bank_m])
            em.op("dve", lambda e, psl_m=psl_m, bsl=bsl: e.tensor_tensor(
                out=gsil[:], in0=psl_m, in1=bsl, op=ALU.add),
                reads=[bank_m, bbs], writes=[b_gsil])
            em.op("act", lambda e: e.activation(out=gsil[:], in_=gsil[:], func=AF.Silu, bias=p.zero1[:]),
                  reads=[b_gsil], writes=[b_gsil])
            em.op("dve", lambda e, psl_s=psl_s, osl=osl: e.tensor_tensor(
                out=osl, in0=psl_s, in1=gsil[:], op=ALU.mult),
                reads=[bank_s, b_gsil], writes=[bot])
            em.op("dve", lambda e, osl=osl, xsl=xsl: e.tensor_tensor(
                out=osl, in0=osl, in1=xsl, op=ALU.add),
                reads=[bxs, bot], writes=[bot])
            em.dma("sp", "dst" + str(par), lambda e, ob=ob, ti=ti, osl=osl: e.dma_start(
                out=p.d_x1[ti * 128:(ti + 1) * 128, ob * 512:(ob + 1) * 512], in_=osl),
                reads=[bot], writes=[p.b_x1d])
    ar.free(tabB2, bslice, xslice, gsil, ot, Y, B0, h0T)
    ar.free(h0, gate, thr, m8, i8, wn)
    ar.free(*[lnscratch[i] for i in (0, 2, 4, 6)])



def build_l1(p, standalone=False):
    nc, em, ar = p.nc, p.em, p.ar
    if standalone:
        p.d_x1 = nc.dram_tensor("x1_in", [NTOK, D], F32, kind="ExternalInput").ap()
        p.b_x1d = Buf("x1_in")
    d_x1, b_x1d = p.d_x1, p.b_x1d

    p.d_out = nc.dram_tensor("out", [NTOK, D], F32, kind="ExternalOutput").ap()
    p.b_outd = Buf("out")
    d_wq = nc.dram_tensor("w_qkv", [QKV_NBLK, 128, DC * 512], F32, kind="ExternalInput").ap()
    d_bq = nc.dram_tensor("b_qkv", [1, QKV_COLS], F32, kind="ExternalInput").ap()
    d_wo = nc.dram_tensor("w_wo", [8, 128, DC * 512], F32, kind="ExternalInput").ap()
    d_bo = nc.dram_tensor("b_wo", [1, D], F32, kind="ExternalInput").ap()
    d_w1 = nc.dram_tensor("w_w1", [8, 128, DC * 512], F32, kind="ExternalInput").ap()
    d_b1 = nc.dram_tensor("b_w1", [1, D], F32, kind="ExternalInput").ap()
    d_B1 = nc.dram_tensor("w_B1", [128, D], F32, kind="ExternalInput").ap()
    d_lw = nc.dram_tensor("lora_w", [1, 8], F32, kind="ExternalInput").ap()
    d_rc = nc.dram_tensor("rope_cos", [NTOK, HEADS * 64], BF16, kind="ExternalInput").ap()
    d_rs = nc.dram_tensor("rope_sin", [NTOK, HEADS * 64], BF16, kind="ExternalInput").ap()
    d_rck = nc.dram_tensor("rope_cos_k", [NTOK, 512], BF16, kind="ExternalInput").ap()
    d_rsk = nc.dram_tensor("rope_sin_k", [NTOK, 512], BF16, kind="ExternalInput").ap()
    d_mask2 = nc.dram_tensor("mask2", [128, 256], F32, kind="ExternalInput").ap()
    d_xn1T = nc.dram_tensor("xn1T_d", [NTILE, 128, DC * 128], BF16, kind="Internal").ap()
    b_xn1Td = Buf("xn1T_d")
    d_qT = nc.dram_tensor("qT_d", [NTILE, 128, HEADS * 128], BF16, kind="Internal").ap()
    b_qTd = Buf("qT_d")
    d_qkv = nc.dram_tensor("qkv_d", [NTOK, QKVD], BF16, kind="Internal").ap()
    b_qkvd = Buf("qkv_d")
    kvkind = "ExternalOutput" if DBG.get("kv") else "Internal"
    d_ccin = nc.dram_tensor("cc_in", [NTOK, 2 * KVDIM], BF16, kind=kvkind).ap()
    b_ccin = Buf("cc_in")
    d_ccout = nc.dram_tensor("cc_out", [2 * NTOK, 2 * KVDIM], BF16, kind=kvkind).ap()
    b_ccout = Buf("cc_out")

    h1T = ar.alloc([128, NTILE, 128], BF16, "h1T")
    b_h1T = Buf("h1T")
    h1res = ar.alloc([128, NTILE, 128], F32, "h1res")
    b_h1res = Buf("h1res")
    masks = ar.alloc([128, 256], F32, "masks")
    b_masks = Buf("masks")
    em.dma("sp", "dsp0", lambda e: e.dma_start(out=masks[:], in_=d_mask2), writes=[b_masks])
    coefb = ar.alloc([128, 8], F32, "coefb")
    b_coefb = Buf("coefb")
    lwrow = ar.alloc([1, 8], BF16, "lwrow")
    b_lwrow = Buf("lwrow")
    em.dma("pool", "dw3", lambda e: e.dma_start(out=lwrow[:], in_=d_lw), writes=[b_lwrow])
    em.mm_group([lambda e: e.matmul(p.ps_b5[:, 0:8], p.ones1[:], lwrow[:], start=True, stop=True)],
                reads=[p.b_ones, b_lwrow], writes=[p.b_b5])
    em.op("dve", lambda e: e.tensor_copy(out=coefb[:], in_=p.ps_b5[:, 0:8]),
          reads=[p.b_b5], writes=[b_coefb])

    lnscr = (ar.alloc([128, 48], F32, "stats1"), Buf("stats1"),
             ar.alloc([128, 2], F32, "mv1"), Buf("mv1"),
             ar.alloc([128, 1], F32, "rstd1"), Buf("rstd1"),
             ar.alloc([128, 1], F32, "nbias1"), Buf("nbias1"))
    xn1T = ar.alloc([128, DC, NTOK], BF16, "xn1Tq")
    b_xn1T = Buf("xn1Tq")
    xa = ar.alloc([128, 2, D], F32, "x1tile")
    b_xa = [Buf("x1a0"), Buf("x1a1")]
    xn1 = ar.alloc([128, D], BF16, "xn1")
    b_xn1 = Buf("xn1")
    for ti in range(NTILE):
        xab, b_x = xa[:, ti % 2, :], b_xa[ti % 2]
        em.dma("sp", "dsp" + str(ti % 2), lambda e, ti=ti, xab=xab: e.dma_start(
            out=xab, in_=d_x1[ti * 128:(ti + 1) * 128, :]), reads=[b_x1d], writes=[b_x])
        layernorm_tile(p, xab, b_x, xn1[:], b_xn1, lnscr)
        for dc in range(DC):
            p.trans128(xn1[:, dc * 128:(dc + 1) * 128], xn1T[:, dc, ti * 128:(ti + 1) * 128],
                       b_xn1, b_xn1T, dc)
        em.dma("sp", "dst0", lambda e, ti=ti: e.dma_start(
            out=d_xn1T[ti].rearrange("p (c n) -> p c n", c=DC),
            in_=xn1T[:, :, ti * 128:(ti + 1) * 128]),
            reads=[b_xn1T], writes=[b_xn1Td])
    ar.free(xa, xn1)
    em.barrier()
    wq = ar.alloc([128, 2, DC, 512], BF16, "wqblk")
    b_wq = [Buf("wq0"), Buf("wq1")]
    brow = ar.alloc([1, QKV_COLS], BF16, "brow1")
    b_brow = Buf("brow1")
    evq = ar.alloc([128, 2, 512], BF16, "evq")
    b_evq = [Buf("evq0"), Buf("evq1")]
    em.dma("pool", "dw2", lambda e: e.dma_start(out=brow[:], in_=d_bq), writes=[b_brow])
    for blk in range(QKV_NBLK):
        wb, bw = wq[:, blk % 2], b_wq[blk % 2]
        em.dma("pool", "dw" + str(blk % 2), lambda e, blk=blk, wb=wb: e.dma_start(
            out=wb, in_=d_wq[blk].rearrange("p (c n) -> p c n", c=DC)),
            writes=[bw])
        for ti in range(NTILE):
            rb4 = (blk * NTILE + ti) % 4
            bank = p.b_acc[rb4]
            psl = p.ps_acc[:, rb4 * 512:(rb4 + 1) * 512]
            mms = []
            for kc in range(DC):
                mms.append(lambda e, kc=kc, ti=ti, wb=wb, psl=psl: e.matmul(
                    psl, xn1T[:, kc, ti * 128:(ti + 1) * 128], wb[:, kc, :],
                    start=(kc == 0), stop=False))
            mms.append(lambda e, blk=blk, psl=psl: e.matmul(
                psl, p.ones1[:], brow[:, blk * 512:(blk + 1) * 512], start=False, stop=True))
            em.mm_group(mms, reads=[b_xn1T, bw, p.b_ones, b_brow], writes=[bank])
            if blk < 12:
                ev, bev = evq[:, ti % 2, :], b_evq[ti % 2]
                em.op("act", lambda e, psl=psl, ev=ev: e.activation(
                    out=ev, in_=psl, func=AF.Identity, bias=p.zero1[:]),
                    reads=[bank], writes=[bev])
                em.dma("sp", "dst" + str(ti % 2), lambda e, blk=blk, ti=ti, ev=ev: e.dma_start(
                    out=d_qkv[ti * 128:(ti + 1) * 128, blk * 512:(blk + 1) * 512], in_=ev),
                    reads=[bev], writes=[b_qkvd])
            else:
                em.op("act", lambda e, ti=ti, psl=psl: e.activation(
                    out=h1res[:, ti, :], in_=psl[:, 0:128], func=AF.Identity, bias=p.zero1[:]),
                    reads=[bank], writes=[b_h1res])
    ar.free(wq, brow, evq, xn1T)
    ar.free(*[lnscr[i] for i in (0, 2, 4, 6)])
    em.barrier()

    qkvt = ar.alloc([128, 2, QKVD], BF16, "qkvt")
    b_qkvt = [Buf("qkvt0"), Buf("qkvt1")]
    crep = ar.alloc([128, 2048], BF16, "crep")
    srep = ar.alloc([128, 2048], BF16, "srep")
    b_crep, b_srep = Buf("crep"), Buf("srep")
    crepk = ar.alloc([128, 512], BF16, "crepk")
    srepk = ar.alloc([128, 512], BF16, "srepk")
    b_crepk, b_srepk = Buf("crepk"), Buf("srepk")
    qrot = ar.alloc([128, QDIM], BF16, "qrot")
    b_qrot = Buf("qrot")
    krot = ar.alloc([128, KVDIM], BF16, "krot")
    b_krot = Buf("krot")
    tmpA = ar.alloc([128, 2048], F32, "tmpA")
    tmpB = ar.alloc([128, 2048], F32, "tmpB")
    b_tA, b_tB = Buf("tmpA"), Buf("tmpB")
    stg = ar.alloc([128, HEADS, 128], BF16, "stg")
    b_stg = Buf("stg")
    h1s = ar.alloc([128, 128], BF16, "h1s")
    b_h1s = Buf("h1s")
    for ti in range(NTILE):
        qk, bqk = qkvt[:, ti % 2, :], b_qkvt[ti % 2]
        em.dma("sp", "dsp" + str(ti % 2), lambda e, ti=ti, qk=qk: e.dma_start(
            out=qk, in_=d_qkv[ti * 128:(ti + 1) * 128, :]),
            reads=[b_qkvd], writes=[bqk])
        em.dma("sp", "dsp2", lambda e, ti=ti: e.dma_start(
            out=crep[:], in_=d_rc[ti * 128:(ti + 1) * 128, 0:2048]), writes=[b_crep])
        em.dma("sp", "dsp3", lambda e, ti=ti: e.dma_start(
            out=srep[:], in_=d_rs[ti * 128:(ti + 1) * 128, 0:2048]), writes=[b_srep])
        em.dma("sp", "dsp2", lambda e, ti=ti: e.dma_start(
            out=crepk[:], in_=d_rck[ti * 128:(ti + 1) * 128, :]), writes=[b_crepk])
        em.dma("sp", "dsp3", lambda e, ti=ti: e.dma_start(
            out=srepk[:], in_=d_rsk[ti * 128:(ti + 1) * 128, :]), writes=[b_srepk])
        em.op("dve", lambda e, qk=qk: e.tensor_tensor(out=tmpA[:], in0=qk[0:128, 0:QDIM:2], in1=crep[:], op=ALU.mult),
              reads=[bqk, b_crep], writes=[b_tA])
        em.op("dve", lambda e, qk=qk: e.tensor_tensor(out=tmpB[:], in0=qk[0:128, 1:QDIM:2], in1=srep[:], op=ALU.mult),
              reads=[bqk, b_srep], writes=[b_tB])
        em.op("dve", lambda e: e.tensor_tensor(out=qrot[:, 0:QDIM:2], in0=tmpA[:], in1=tmpB[:], op=ALU.subtract),
              reads=[b_tA, b_tB], writes=[b_qrot])
        em.op("dve", lambda e, qk=qk: e.tensor_tensor(out=tmpA[:], in0=qk[0:128, 0:QDIM:2], in1=srep[:], op=ALU.mult),
              reads=[bqk, b_srep], writes=[b_tA])
        em.op("dve", lambda e, qk=qk: e.tensor_tensor(out=tmpB[:], in0=qk[0:128, 1:QDIM:2], in1=crep[:], op=ALU.mult),
              reads=[bqk, b_crep], writes=[b_tB])
        em.op("dve", lambda e: e.tensor_tensor(out=qrot[:, 1:QDIM:2], in0=tmpA[:], in1=tmpB[:], op=ALU.add),
              reads=[b_tA, b_tB], writes=[b_qrot])
        em.op("dve", lambda e, qk=qk: e.tensor_tensor(
            out=tmpA[:, 0:512], in0=qk[0:128, QDIM:QDIM + KVDIM:2], in1=crepk[:], op=ALU.mult),
            reads=[bqk, b_crepk], writes=[b_tA])
        em.op("dve", lambda e, qk=qk: e.tensor_tensor(
            out=tmpB[:, 0:512], in0=qk[0:128, QDIM + 1:QDIM + KVDIM:2], in1=srepk[:], op=ALU.mult),
            reads=[bqk, b_srepk], writes=[b_tB])
        em.op("dve", lambda e: e.tensor_tensor(
            out=krot[:, 0:KVDIM:2], in0=tmpA[:, 0:512], in1=tmpB[:, 0:512], op=ALU.subtract),
            reads=[b_tA, b_tB], writes=[b_krot])
        em.op("dve", lambda e, qk=qk: e.tensor_tensor(
            out=tmpA[:, 0:512], in0=qk[0:128, QDIM:QDIM + KVDIM:2], in1=srepk[:], op=ALU.mult),
            reads=[bqk, b_srepk], writes=[b_tA])
        em.op("dve", lambda e, qk=qk: e.tensor_tensor(
            out=tmpB[:, 0:512], in0=qk[0:128, QDIM + 1:QDIM + KVDIM:2], in1=crepk[:], op=ALU.mult),
            reads=[bqk, b_crepk], writes=[b_tB])
        em.op("dve", lambda e: e.tensor_tensor(
            out=krot[:, 1:KVDIM:2], in0=tmpA[:, 0:512], in1=tmpB[:, 0:512], op=ALU.add),
            reads=[b_tA, b_tB], writes=[b_krot])
        em.dma("sp", "dst1", lambda e, ti=ti: e.dma_start(
            out=d_ccin[ti * 128:(ti + 1) * 128, 0:KVDIM], in_=krot[:]),
            reads=[b_krot], writes=[b_ccin])
        em.dma("sp", "dst1", lambda e, ti=ti, qk=qk: e.dma_start(
            out=d_ccin[ti * 128:(ti + 1) * 128, KVDIM:2 * KVDIM], in_=qk[0:128, QDIM + KVDIM:QKVD]),
            reads=[bqk], writes=[b_ccin])
        for ee in range(8):
            em.op("dve", lambda e, ti=ti, ee=ee: e.tensor_scalar(
                out=h1s[:, ee * 16:(ee + 1) * 16], in0=h1res[:, ti, ee * 16:(ee + 1) * 16],
                scalar1=coefb[:, ee:ee + 1], scalar2=2.0, op0=ALU.mult, op1=ALU.mult),
                reads=[b_h1res, b_coefb], writes=[b_h1s])
        p.trans128(h1s[:], h1T[:, ti, :], b_h1s, b_h1T, ti)
        for hh in range(HEADS):
            p.trans128(qrot[:, hh * 128:(hh + 1) * 128], stg[:, hh, :], b_qrot, b_stg, hh)
        em.dma("sp", "dst0", lambda e, ti=ti: e.dma_start(
            out=d_qT[ti].rearrange("p (c n) -> p c n", c=HEADS), in_=stg[:]),
            reads=[b_stg], writes=[b_qTd])
        if DBG.get("qkv") and ti == 0:
            p.tap("qkv", qk, [128, QKVD], BF16, [bqk], force=True)
            p.tap("qrot", qrot[:], [128, QDIM], BF16, [b_qrot], force=True)
    ar.free(qkvt, crep, srep, crepk, srepk, qrot, krot, tmpA, tmpB, stg, h1s, h1res)
    em.barrier()

    em.op("pool", lambda e: e.collective_compute(
        "AllGather", ALU.bypass,
        replica_groups=[[0, 1], [2, 3], [4, 5], [6, 7]],
        ins=[d_ccin.opt()], outs=[d_ccout.opt()]),
        reads=[b_ccin], writes=[b_ccout])
    KT = ar.alloc([128, KVH, S], BF16, "KT")
    b_KT = Buf("KTfull")
    Vsb = ar.alloc([128, 16, KVDIM], BF16, "Vsb")
    b_V = Buf("Vsb")
    ktmp = ar.alloc([128, 2, KVDIM], BF16, "ktmp")
    b_kt = [Buf("kt0"), Buf("kt1")]
    for rp in range(2):
        for tp in range(NTILE):
            ci = 2 * tp + rp
            row0 = rp * NTOK + tp * 128
            kb, bkb = ktmp[:, ci % 2, :], b_kt[ci % 2]
            em.dma("sp", "dsp" + str(ci % 2), lambda e, row0=row0, kb=kb: e.dma_start(
                out=kb, in_=d_ccout[row0:row0 + 128, 0:KVDIM]),
                reads=[b_ccout], writes=[bkb])
            for kv in range(KVH):
                p.trans128(kb[:, kv * 128:(kv + 1) * 128], KT[:, kv, ci * 128:(ci + 1) * 128],
                           bkb, b_KT, kv)
            em.dma("sp", "dsp" + str(2 + ci % 2), lambda e, row0=row0, ci=ci: e.dma_start(
                out=Vsb[:, ci, :], in_=d_ccout[row0:row0 + 128, KVDIM:2 * KVDIM]),
                reads=[b_ccout], writes=[b_V])
    ar.free(ktmp)
    em.barrier()

    s1T = ar.alloc([128, HEADS, NTOK], BF16, "s1T")
    b_s1T = Buf("s1T")
    qt = ar.alloc([128, 2, HEADS, 128], BF16, "qt")
    b_qt = [Buf("qt0"), Buf("qt1")]
    mask2b = ar.alloc([128, 256], BF16, "mask2b")
    b_mask2b = Buf("mask2b")
    em.op("dve", lambda e: e.tensor_copy(out=mask2b[:], in_=masks[:]), reads=[b_masks], writes=[b_mask2b])
    probs = ar.alloc([128, 4, 2, 512], BF16, "probs")
    b_pr = [[Buf(f"pr{s}{i}") for i in range(2)] for s in range(4)]
    PT = ar.alloc([128, 4, 2, 128], BF16, "PT")
    b_PT = [[Buf(f"PT{s}{i}") for i in range(2)] for s in range(4)]
    negm = ar.alloc([128, 4, 2], F32, "negm")
    b_negm = [Buf(f"negm{s}") for s in range(4)]
    lrow = ar.alloc([128, 4], F32, "lrow")
    b_lr = [Buf(f"lr{s}") for s in range(4)]
    acc = ar.alloc([128, 4, 128], F32, "acch")
    b_acc_h = [Buf(f"acc{s}") for s in range(4)]
    cm = ar.alloc([128, 4, 2], F32, "cm")
    b_cm = [[Buf(f"cm{s}{i}") for i in range(2)] for s in range(4)]
    alph = ar.alloc([128, 4, 2], F32, "alph")
    b_al = [[Buf(f"al{s}{i}") for i in range(2)] for s in range(4)]
    dmm = ar.alloc([128, 4, 2], F32, "dmm")
    b_dm = [[Buf(f"dm{s}{i}") for i in range(2)] for s in range(4)]
    lc = ar.alloc([128, 4, 2], F32, "lc")
    b_lc = [[Buf(f"lc{s}{i}") for i in range(2)] for s in range(4)]
    linv = ar.alloc([128, 4], F32, "linv")
    b_li = [Buf(f"li{s}") for s in range(4)]
    sv = ar.alloc([128, 4, 128], BF16, "sv")
    b_sv = [Buf(f"sv{s}") for s in range(4)]

    def head_pipeline(ti, hh, s, qb, bqb):
        nk = (ti + 1) * 256
        nb = (nk + 511) // 512
        kv = hh // GRP
        bng, blr, bah = b_negm[s], b_lr[s], b_acc_h[s]
        ngl = negm[:, s, :]
        bank = p.b_acc[s]
        pvps = p.ps_acc[:, s * 512:s * 512 + 512]
        bpv = bank
        cur, new = 0, 1
        for c2 in range(nb):
            w = min(512, nk - 512 * c2)
            ci = c2 % 2
            psl = p.ps_acc[:, s * 512:s * 512 + w]
            prb, bprb = probs[:, s, ci, :], b_pr[s][ci]
            cml, bcml = cm[:, s, ci:ci + 1], b_cm[s][ci]
            dml = dmm[:, s, ci:ci + 1]
            all_, bal = alph[:, s, ci:ci + 1], b_al[s][ci]
            lcl, blcl = lc[:, s, ci:ci + 1], b_lc[s][ci]
            first, last = (c2 == 0), (c2 == nb - 1)
            mms = [lambda e, hh=hh, kv=kv, w=w, c2=c2, psl=psl, qb=qb, last=last: e.matmul(
                psl, qb[:, hh, :], KT[:, kv, c2 * 512:c2 * 512 + w],
                start=True, stop=not last)]
            if last:
                mms.append(lambda e, psl=psl, w=w: e.matmul(
                    psl[:, w - 256:w], p.ident[:], mask2b[:], start=False, stop=True))
            em.mm_group(mms, reads=[bqb, b_KT, p.b_ident, b_mask2b], writes=[bank])
            yield
            em.op("dve", lambda e, psl=psl, cml=cml: e.reduce_max(
                out=cml, in_=psl, axis=AX, negate=True), reads=[bank], writes=[bcml])
            if first:
                em.op("dve", lambda e, ngl=ngl, cml=cml, new=new: e.tensor_copy(
                    out=ngl[:, new:new + 1], in_=cml), reads=[bcml], writes=[bng])
            else:
                em.op("dve", lambda e, ngl=ngl, cml=cml, cur=cur, new=new: e.tensor_tensor(
                    out=ngl[:, new:new + 1], in0=ngl[:, cur:cur + 1], in1=cml, op=ALU.min),
                    reads=[bcml, bng], writes=[bng])
                em.op("dve", lambda e, ngl=ngl, dml=dml, cur=cur, new=new: e.tensor_tensor(
                    out=dml, in0=ngl[:, new:new + 1], in1=ngl[:, cur:cur + 1], op=ALU.subtract),
                    reads=[bng], writes=[b_dm[s][ci]])
            yield
            if not first:
                em.op("act", lambda e, all_=all_, dml=dml: e.activation(
                    out=all_, in_=dml, func=AF.Exp, bias=p.zero1[:]),
                    reads=[b_dm[s][ci], p.b_zero1], writes=[bal])
            em.op("act", lambda e, psl=psl, prb=prb, w=w, ngl=ngl, new=new, lcl=lcl: e.activation(
                out=prb[0:128, 0:w], in_=psl, func=AF.Exp, bias=ngl[:, new:new + 1],
                accum_out=lcl),
                reads=[bank, bng], writes=[bprb, blcl])
            yield
            if first:
                em.op("dve", lambda e, lcl=lcl: e.tensor_copy(
                    out=lrow[:, s:s + 1], in_=lcl), reads=[blcl], writes=[blr])
            else:
                em.op("dve", lambda e, all_=all_, lcl=lcl: e.scalar_tensor_tensor(
                    out=lrow[:, s:s + 1], in0=lrow[:, s:s + 1], scalar=all_, in1=lcl,
                    op0=ALU.mult, op1=ALU.add),
                    reads=[bal, blcl, blr], writes=[blr])
            nch = (w + 127) // 128
            for kc in range(nch):
                pts, bpts = PT[:, s, kc % 2, :], b_PT[s][kc % 2]
                p.trans128(prb[0:128, kc * 128:(kc + 1) * 128], pts, bprb, bpts, s,
                           copy_eng="act" if kc % 2 else "dve")
                em.mm_group([lambda e, kc=kc, kv=kv, c2=c2, pts=pts, nch=nch, pvps=pvps: e.matmul(
                    pvps[:, 0:128], pts, Vsb[:, 4 * c2 + kc, kv * 128:(kv + 1) * 128],
                    start=(kc == 0), stop=(kc == nch - 1), skip_group_check=True)],
                    reads=[bpts, b_V], writes=[bpv])
                if kc % 2 == 1 or kc == nch - 1:
                    yield
            if first:
                em.op("dve", lambda e, pvps=pvps: e.tensor_copy(
                    out=acc[:, s, :], in_=pvps[:, 0:128]),
                    reads=[bpv], writes=[bah])
            else:
                em.op("dve", lambda e, all_=all_, pvps=pvps: e.scalar_tensor_tensor(
                    out=acc[:, s, :], in0=acc[:, s, :], scalar=all_, in1=pvps[:, 0:128],
                    op0=ALU.mult, op1=ALU.add),
                    reads=[bal, bpv, bah], writes=[bah])
            cur, new = new, cur
            yield
        em.op("dve", lambda e: e.reciprocal(out=linv[:, s:s + 1], in_=lrow[:, s:s + 1]),
              reads=[blr], writes=[b_li[s]])
        em.op("act", lambda e: e.activation(
            out=sv[:, s, :], in_=acc[:, s, :], func=AF.Identity, scale=linv[:, s:s + 1],
            bias=p.zero1[:]),
            reads=[bah, b_li[s], p.b_zero1], writes=[b_sv[s]])
        yield
        p.trans128(sv[:, s, :], s1T[:, hh, ti * 128:(ti + 1) * 128], b_sv[s], b_s1T, s)
        yield

    for ti in range(NTILE):
        qb, bqb = qt[:, ti % 2], b_qt[ti % 2]
        em.dma("sp", "dsp" + str(ti % 2), lambda e, ti=ti, qb=qb: e.dma_start(
            out=qb, in_=d_qT[ti].rearrange("p (c n) -> p c n", c=HEADS)),
            reads=[b_qTd], writes=[bqb])
        gens = []
        nxt_h = 0
        for s in range(4):
            g = head_pipeline(ti, nxt_h, s, qb, bqb)
            nxt_h += 1
            for _ in range(2 * (3 - s)):
                try:
                    next(g)
                except StopIteration:
                    break
            gens.append((g, s))
        while gens:
            keep = []
            for g, s in gens:
                try:
                    next(g)
                    keep.append((g, s))
                except StopIteration:
                    if nxt_h < HEADS:
                        keep.append((head_pipeline(ti, nxt_h, s, qb, bqb), s))
                        nxt_h += 1
            gens = keep
    ar.free(qt, probs, PT, negm, lrow, acc, cm, alph, dmm, lc, linv, sv, KT, Vsb, masks, mask2b)
    if DBG.get("s1T"):
        p.tap("s1T", s1T[:], [128, HEADS * NTOK], BF16, [b_s1T])
    em.barrier()

    xn1Ts = ar.alloc([128, DC, NTOK], BF16, "xn1Ts")
    b_xn1Ts = Buf("xn1Ts")
    for ti in range(NTILE):
        em.dma("sp", "dsp" + str(ti % 2), lambda e, ti=ti: e.dma_start(
            out=xn1Ts[:, :, ti * 128:(ti + 1) * 128], in_=d_xn1T[ti].rearrange("p (c n) -> p c n", c=DC)),
            reads=[b_xn1Td], writes=[b_xn1Ts])
    wblk = ar.alloc([128, DC, 512], BF16, "wblk1")
    b_wb = Buf("wblk1")
    brow_o = ar.alloc([1, 512], BF16, "brow_o")
    b_bro = Buf("brow_o")
    brow_1 = ar.alloc([1, 512], BF16, "brow_1")
    b_br1 = Buf("brow_1")
    B1sb = ar.alloc([128, D], BF16, "B1sb")
    b_B1 = Buf("B1sb")
    proj_sb = ar.alloc([128, NTILE, 512], BF16, "proj_sb")
    b_proj = Buf("proj_sb")
    gsil = ar.alloc([128, 512], F32, "gsil1")
    b_gs = Buf("gsil1")
    xsl = ar.alloc([128, 2, 512], F32, "xsl1")
    b_xsl = [Buf("xsl0"), Buf("xsl1")]
    osl = ar.alloc([128, 2, 512], F32, "osl1")
    b_osl = [Buf("osl0"), Buf("osl1")]
    em.dma("pool", "dw3", lambda e: e.dma_start(out=B1sb[:], in_=d_B1), writes=[b_B1])
    for ob in range(8):
        em.dma("pool", "dw2", lambda e, ob=ob: e.dma_start(
            out=brow_o[:], in_=d_bo[:, ob * 512:(ob + 1) * 512]), writes=[b_bro])
        em.dma("pool", "dw2", lambda e, ob=ob: e.dma_start(
            out=brow_1[:], in_=d_b1[:, ob * 512:(ob + 1) * 512]), writes=[b_br1])
        em.dma("pool", "dw0", lambda e, ob=ob: e.dma_start(
            out=wblk[:], in_=d_wo[ob].rearrange("p (c n) -> p c n", c=DC)),
            writes=[b_wb])
        for ti in range(NTILE):
            rb4 = (ob * NTILE + ti) % 4
            bank = p.b_acc[rb4]
            psl = p.ps_acc[:, rb4 * 512:(rb4 + 1) * 512]
            mms = []
            for kc in range(DC):
                mms.append(lambda e, kc=kc, ti=ti, psl=psl: e.matmul(
                    psl, s1T[:, kc, ti * 128:(ti + 1) * 128], wblk[:, kc, :],
                    start=(kc == 0), stop=False))
            mms.append(lambda e, ob=ob, psl=psl: e.matmul(
                psl, p.ones1[:], brow_o[:], start=False, stop=True))
            em.mm_group(mms, reads=[b_s1T, b_wb, p.b_ones, b_bro], writes=[bank])
            em.op("act", lambda e, ti=ti, psl=psl: e.activation(
                out=proj_sb[:, ti, :], in_=psl, func=AF.Identity, bias=p.zero1[:]),
                reads=[bank], writes=[b_proj])
        em.dma("pool", "dw1", lambda e, ob=ob: e.dma_start(
            out=wblk[:], in_=d_w1[ob].rearrange("p (c n) -> p c n", c=DC)),
            writes=[b_wb])
        for ti in range(NTILE):
            par = ti % 2
            bank = p.b_acc[2 + par]
            psl = p.ps_acc[:, (2 + par) * 512:(3 + par) * 512]
            xs_, bxs = xsl[:, par], b_xsl[par]
            os_, bos = osl[:, par], b_osl[par]
            em.dma("sp", "dsp" + str(2 + par), lambda e, ob=ob, ti=ti, xs_=xs_: e.dma_start(
                out=xs_, in_=d_x1[ti * 128:(ti + 1) * 128, ob * 512:(ob + 1) * 512]),
                reads=[b_x1d], writes=[bxs])
            mms = []
            for kc in range(DC):
                mms.append(lambda e, kc=kc, ti=ti, psl=psl: e.matmul(
                    psl, xn1Ts[:, kc, ti * 128:(ti + 1) * 128], wblk[:, kc, :],
                    start=(kc == 0), stop=False))
            mms.append(lambda e, ob=ob, psl=psl: e.matmul(
                psl, p.ones1[:], brow_1[:], start=False, stop=False))
            mms.append(lambda e, ob=ob, ti=ti, psl=psl: e.matmul(
                psl, h1T[:, ti, :], B1sb[:, ob * 512:(ob + 1) * 512], start=False, stop=True))
            em.mm_group(mms, reads=[b_xn1Ts, b_wb, p.b_ones, b_br1, b_h1T, b_B1], writes=[bank])
            em.op("act", lambda e, psl=psl: e.activation(
                out=gsil[:], in_=psl, func=AF.Silu, bias=p.zero1[:]),
                reads=[bank], writes=[b_gs])
            em.op("dve", lambda e, ti=ti, os_=os_: e.tensor_tensor(
                out=os_, in0=proj_sb[:, ti, :], in1=gsil[:], op=ALU.mult),
                reads=[b_proj, b_gs], writes=[bos])
            em.op("dve", lambda e, os_=os_, xs_=xs_: e.tensor_tensor(
                out=os_, in0=os_, in1=xs_, op=ALU.add),
                reads=[bxs, bos], writes=[bos])
            em.dma("sp", "dst" + str(par), lambda e, ob=ob, ti=ti, os_=os_: e.dma_start(
                out=p.d_out[ti * 128:(ti + 1) * 128, ob * 512:(ob + 1) * 512], in_=os_),
                reads=[bos], writes=[p.b_outd])
    ar.free(xn1Ts, wblk, brow_o, brow_1, B1sb, proj_sb, gsil, xsl, osl, s1T, h1T, coefb, lwrow)


def finalize(p):
    p.em.barrier()


def build_nc(core, layers=(0,)):
    nc = bass.Bass("TRN2", target_bir_lowering=False, debug=False)
    es = ExitStack()
    with es:
        p = Prog(nc, es, core)
        p.prelude()
        if 0 in layers:
            build_l0(p)
        if 1 in layers:
            build_l1(p, standalone=(0 not in layers))
        finalize(p)
        block = es.enter_context(nc.Block())
        p.em.replay(block)
    return nc


def core_in_maps(inputs, core, layers=(0, 1), x1_shards=None):
    t = host_tables()
    w = host_weights(inputs)
    s, tiles = shard_rows(core)
    rows = np.concatenate([np.arange(tt * 128, (tt + 1) * 128) for tt in tiles])
    im = {"ident": t["ident"], "identf": t["ident"].astype(np.float32)}
    if 0 in layers:
        im.update({
            "x_in": shard_x(inputs["L0_attn_x"], core),
            "cosA": t["cosA"], "sinA": t["sinA"],
            "cosA_t": t["cosA_t"], "sinA_t": t["sinA_t"],
            "cosB_t": t["cosB_t"], "sinB_t": t["sinB_t"],
            "w_b2": w["w_b2"], "b_b2": w["b_b2"], "w_B0": w["w_B0"],
        })
    if 1 in layers:
        parity = core % 2
        q = np.arange(128)[:, None]
        k2 = np.arange(256)[None, :]
        mask2 = np.where(k2 <= q + 128 * parity, 0.0, MAXNEG).astype(np.float32)
        rc = t["rope_cos"][rows].astype(np.float32)
        rs = t["rope_sin"][rows].astype(np.float32)
        rcq = (rc * ATT_SCALE).astype(BF)
        rsq = (rs * ATT_SCALE).astype(BF)
        im.update({
            "rope_cos": np.ascontiguousarray(rcq),
            "rope_sin": np.ascontiguousarray(rsq),
            "rope_cos_k": np.ascontiguousarray(rc[:, 0:512].astype(BF)),
            "rope_sin_k": np.ascontiguousarray(rs[:, 0:512].astype(BF)),
            "mask2": mask2,
            "w_qkv": w["w_qkv"], "b_qkv": w["b_qkv"],
            "w_wo": w["w_wo"], "b_wo": w["b_wo"],
            "w_w1": w["w_w1"], "b_w1": w["b_w1"],
            "w_B1": w["w_B1"], "lora_w": w["lora_w"],
        })
        if 0 not in layers:
            im["x1_in"] = x1_shards[core]
    return im


FUSED = True
_NC_CACHE = {}


def _get_nc(layers):
    key = tuple(layers)
    if key not in _NC_CACHE:
        _NC_CACHE[key] = build_nc(0, layers=key)
    return _NC_CACHE[key]


def kernel(**inputs):
    if FUSED:
        nc = _get_nc((0, 1))
        im = [core_in_maps(inputs, c, (0, 1)) for c in range(NCORE)]
        res = run_bass_kernel_spmd(nc, im, core_ids=list(range(NCORE)))
        return unshard([res.results[c]["out"] for c in range(NCORE)])
    global DBG
    old = dict(DBG)
    DBG.clear()
    DBG["l0_out"] = True
    nc0 = _get_nc((0,))
    DBG.clear()
    DBG.update(old)
    im0 = [core_in_maps(inputs, c, (0,)) for c in range(NCORE)]
    r0 = run_bass_kernel_spmd(nc0, im0, core_ids=list(range(NCORE)))
    x1_shards = [np.asarray(r0.results[c]["x1_d"]) for c in range(NCORE)]
    nc1 = _get_nc((1,))
    im1 = [core_in_maps(inputs, c, (1,), x1_shards=x1_shards) for c in range(NCORE)]
    r1 = run_bass_kernel_spmd(nc1, im1, core_ids=list(range(NCORE)))
    return unshard([r1.results[c]["out"] for c in range(NCORE)])
